# version 35
# baseline (speedup 1.0000x reference)
import contextlib
import os
import numpy as np
import ml_dtypes
import concourse.bass as bass
import concourse.mybir as mybir
from concourse.bass_utils import run_bass_kernel_spmd

F32 = mybir.dt.float32
BF16 = mybir.dt.bfloat16
ALU = mybir.AluOpType
ACT = mybir.ActivationFunctionType

D = 1024
EPS = 1e-5
EXPC = 20.0
VA = 322


class Emitter:
    def __init__(self, nc, es):
        self.h = {}
        self.c = {}
        for n in ("dW", "dC", "dX0", "dX1", "dO0", "dO1",
                  "dP0T0", "dP0T1", "dP0T2", "dP0T3",
                  "dP1T0", "dP1T1", "dP1T2", "dP1T3",
                  "sT", "sV", "sS", "sG"):
            self.h[n] = es.enter_context(nc.semaphore(n))
            self.c[n] = 0

    def inc(self, inst, name, k=1):
        inst.then_inc(self.h[name], k)
        self.c[name] += k
        return self.c[name]

    def wait(self, eng, name, val=None):
        v = self.c[name] if val is None else val
        if v > 0:
            eng.wait_ge(self.h[name], v)
        return v


def chunk_cfgs(role, seq_len):
    if role == "full":
        out = []
        for i in range(seq_len // 512):
            out.append(dict(qs=i * 512, W=512, ghost=False,
                            tail_src=("zero" if i == 0 else "prev"),
                            tail_dst="prev", out_row=i * 512))
        return out, seq_len
    if role == 0:
        return [
            dict(qs=2944, W=128, ghost=True, tail_src=None, tail_dst="g",
                 out_row=None),
            dict(qs=0, W=512, ghost=False, tail_src="zero", tail_dst="prev",
                 out_row=0),
            dict(qs=512, W=512, ghost=False, tail_src="prev", tail_dst="prev",
                 out_row=512),
            dict(qs=3072, W=512, ghost=False, tail_src="g", tail_dst="prev",
                 out_row=1024),
            dict(qs=3584, W=512, ghost=False, tail_src="prev", tail_dst="prev",
                 out_row=1536),
        ], 2048
    return [
        dict(qs=896, W=128, ghost=True, tail_src=None, tail_dst="g",
             out_row=None),
        dict(qs=1024, W=512, ghost=False, tail_src="g", tail_dst="prev",
             out_row=0),
        dict(qs=1536, W=512, ghost=False, tail_src="prev", tail_dst="prev",
             out_row=512),
        dict(qs=2048, W=512, ghost=False, tail_src="prev", tail_dst="prev",
             out_row=1024),
        dict(qs=2560, W=512, ghost=False, tail_src="prev", tail_dst="prev",
             out_row=1536),
    ], 2048


TAIL_SLOT = {"zero": 0, "prev": 1, "g": 2}


def build_nc(role="full", seq_len=4096):
    cfgs, out_rows = chunk_cfgs(role, seq_len)
    kv_len = seq_len
    nkvt = kv_len // 128

    nc = bass.Bass("TRN2", target_bir_lowering=False,
                   dynamic_dma_scratch_size=8192)
    dt = lambda n, s, d=F32, k="ExternalInput": nc.dram_tensor(n, s, d, kind=k)
    xkv_d = dt("xkv", [seq_len, D])
    wq_d = dt("wq", [D, D]); wk_d = dt("wk", [D, 256]); wv_d = dt("wv", [D, 256])
    wo_d = dt("wo", [D, D])
    wrT_d = dt("wrT", [D, D]); wkT_d = dt("wkT", [D, D]); wvT_d = dt("wvT", [D, D])
    lng_d = dt("lng", [128, 8]); lnb_d = dt("lnb", [128, 8])
    bq_d = dt("bq", [128, 8]); bk_d = dt("bk", [128, 2]); bv_d = dt("bv", [128, 2])
    bo_d = dt("bo", [1, D])
    mur_d = dt("mur", [128, 8]); muk_d = dt("muk", [128, 8])
    identb_d = dt("identb", [128, 128], BF16)
    trim_d = dt("trim", [128, 128], BF16)
    ones2_d = dt("ones2", [128, 66], BF16)
    onesr_d = dt("onesr", [1, 128], BF16)
    onescol_d = dt("onescol", [128, 1], BF16)
    out_d = dt("out", [out_rows, D], F32, k="ExternalOutput")

    es = contextlib.ExitStack()
    sb = lambda n, s, d: es.enter_context(nc.sbuf_tensor(n, s, d))
    wq = sb("wq_s", [128, 8192], BF16)
    wk = sb("wk_s", [128, 2048], BF16)
    wv = sb("wv_s", [128, 2048], BF16)
    wo = sb("wo_s", [128, 8192], BF16)
    wr = sb("wr_s", [128, 8192], BF16)
    wkf = sb("wkf_s", [128, 8192], BF16)
    wvf = sb("wvf_s", [128, 8192], BF16)
    kfm = sb("kfm", [128, 2 * kv_len], BF16)
    vaug = sb("vaug", [128, nkvt * VA + 32], BF16)
    xb = [sb("xb0", [128, 4096], F32), sb("xb1", [128, 4096], F32)]
    sc8a = sb("sc8a", [128, 4096], BF16)
    fmA = sb("fmA", [128, 4096], BF16)
    fmB = sb("fmB", [128, 4096], BF16)
    xn3f = sb("xn3f", [128, 8 * 513], BF16)
    pt = sb("pt", [128, 2048], BF16)
    o2s = sb("o2s", [128, 512], BF16)
    xmix = sb("xmix", [128, 4096], BF16)
    bnb = sb("bnb", [128, 28], F32)
    cbl = sb("cbl", [128, 16], F32)
    bqc = sb("bqc", [128, 8], F32)
    bkc = sb("bkc", [128, 2], F32)
    bvc = sb("bvc", [128, 2], F32)
    lng = sb("lng_s", [128, 8], F32)
    lnbb = sb("lnbb", [128, 8], BF16)
    murc = sb("murc", [128, 8], F32)
    mukc = sb("mukc", [128, 8], F32)
    murc2 = sb("murc2", [128, 8], F32)
    mukc2 = sb("mukc2", [128, 8], F32)
    tails = sb("tails", [128, 24], BF16)
    identb = sb("identb_s", [128, 128], BF16)
    trim = sb("trim_s", [128, 128], BF16)
    ones2 = sb("ones2_s", [128, 66], BF16)
    onesr = sb("onesr_s", [1, 128], BF16)
    onescol = sb("onescol_s", [128, 1], BF16)
    rzrow = sb("rzrow", [1, 512], BF16)
    rirow = sb("rirow", [1, 512], BF16)
    cwrow = sb("cwrow", [1, 1024], BF16)
    obbf = sb("obbf", [1, 1024], BF16)
    rwsc = sb("rwsc", [1, 1024], F32)
    rzbc = sb("rzbc", [128, 512], BF16)
    qstage = sb("qstage", [128, 1024], BF16)
    nexpc = sb("nexpc", [128, 1], F32)
    ps_s = es.enter_context(nc.psum_tensor("ps_s", [128, 2048], F32))
    ps_o = es.enter_context(nc.psum_tensor("ps_o", [128, 1024], F32))

    E = Emitter(nc, es)
    t, v, s, g, sp = nc.tensor, nc.vector, nc.scalar, nc.gpsimd, nc.sync
    blk = es.enter_context(nc.Block())
    scope_i = [0]

    with es:
        def gdma(dst_ap, src_ap):
            E.inc(g.dma_start(dst_ap, src_ap), "dW", 16)

        gdma(bass.AP(wq, 0, [[8192, 128], [1024, 8], [1, 1024]]),
             bass.AP(wq_d, 0, [[1024, 128], [131072, 8], [1, 1024]]))
        gdma(bass.AP(wk, 0, [[2048, 128], [256, 8], [1, 256]]),
             bass.AP(wk_d, 0, [[256, 128], [32768, 8], [1, 256]]))
        gdma(bass.AP(wv, 0, [[2048, 128], [256, 8], [1, 256]]),
             bass.AP(wv_d, 0, [[256, 128], [32768, 8], [1, 256]]))
        gdma(bass.AP(wo, 0, [[8192, 128], [1024, 8], [1, 1024]]),
             bass.AP(wo_d, 0, [[1024, 128], [131072, 8], [1, 1024]]))
        gdma(bass.AP(wr, 0, [[8192, 128], [1024, 8], [1, 1024]]),
             bass.AP(wrT_d, 0, [[1024, 128], [131072, 8], [1, 1024]]))
        gdma(bass.AP(wkf, 0, [[8192, 128], [1024, 8], [1, 1024]]),
             bass.AP(wkT_d, 0, [[1024, 128], [131072, 8], [1, 1024]]))
        gdma(bass.AP(wvf, 0, [[8192, 128], [1024, 8], [1, 1024]]),
             bass.AP(wvT_d, 0, [[1024, 128], [131072, 8], [1, 1024]]))
        gdma(lnbb[:, :], lnb_d.ap())

        def cdma(dst_ap, src_ap):
            E.inc(sp.dma_start(dst_ap, src_ap), "dC", 16)

        cdma(bqc[:, :], bq_d.ap())
        cdma(bkc[:, :], bk_d.ap())
        cdma(bvc[:, :], bv_d.ap())
        cdma(lng[:, :], lng_d.ap())
        cdma(murc[:, :], mur_d.ap())
        cdma(mukc[:, :], muk_d.ap())
        cdma(rwsc[0:1, :], bo_d.ap())
        cdma(identb[:, :], identb_d.ap())
        cdma(trim[:, :], trim_d.ap())
        cdma(ones2[:, :], ones2_d.ap())
        cdma(onesr[0:1, :], onesr_d.ap())
        cdma(onescol[:, :], onescol_d.ap())

        for eng in (t, v, s, g):
            E.wait(eng, "dW")
            E.wait(eng, "dC")

        if int(os.environ.get("KSTAGE", "99")) == -1:
            E.wait(sp, "dW")
            E.wait(sp, "dC")
            return nc

        nslot = 2 * nkvt
        vw = nkvt * VA + 32
        E.inc(g.memset(vaug[:, 0:32], 0.0), "sG")
        E.inc(g.memset(
            bass.AP(vaug, 32, [[vw, 128], [161, nslot], [1, 1]]), 1.0), "sG")
        E.inc(g.memset(
            bass.AP(vaug, 32 + 160, [[vw, 128], [161, nslot], [1, 1]]), 1.0), "sG")
        E.inc(v.memset(qstage[:, :], 0.0), "sV")
        E.inc(v.tensor_scalar(murc2[:, :], murc[:, :], -1.0, 1.0,
                              ALU.mult, ALU.add), "sV")
        E.inc(v.tensor_scalar(mukc2[:, :], mukc[:, :], -1.0, 1.0,
                              ALU.mult, ALU.add), "sV")
        gset = E.inc(g.memset(
            bass.AP(vaug, 33, [[vw, 128], [161, nslot], [1, 31]]), 0.0), "sG")
        E.inc(v.memset(tails[:, 0:8], 0.0), "sV")
        expc_v = E.inc(v.memset(nexpc[:, :], -EXPC), "sV")

        for hf in range(2):
            for cc in range(8):
                _mm = t.matmul(ps_s[0:1, hf * 512:(hf + 1) * 512],
                         lnbb[:, cc:cc + 1],
                         wo[:, cc * 1024 + hf * 512: cc * 1024 + (hf + 1) * 512],
                         start=(cc == 0), stop=(cc == 7))
        tob = E.inc(_mm, "sT")
        E.wait(v, "sT", tob)
        vv = E.inc(v.tensor_add(rwsc[0:1, :], rwsc[0:1, :], ps_s[0:1, 0:1024]), "sV")
        E.wait(v, "sV", vv)
        vv = E.inc(v.tensor_copy(obbf[0:1, :], rwsc[0:1, :]), "sV")
        E.wait(v, "sV", vv)
        for cc in range(8):
            vv = E.inc(v.tensor_scalar_mul(wo[:, cc * 1024:(cc + 1) * 1024],
                                           wo[:, cc * 1024:(cc + 1) * 1024],
                                           lng[:, cc:cc + 1]), "sV")
        vwo = vv
        E.wait(t, "sV", vwo)
        for hf in range(2):
            for cc in range(8):
                _mm = t.matmul(ps_s[0:1, 1024 + hf * 512:1024 + (hf + 1) * 512],
                         onescol[:, 0:1],
                         wo[:, cc * 1024 + hf * 512: cc * 1024 + (hf + 1) * 512],
                         start=(cc == 0), stop=(cc == 7))
        tcw = E.inc(_mm, "sT")
        E.wait(v, "sT", tcw)
        vprep = E.inc(v.tensor_copy(cwrow[0:1, :], ps_s[0:1, 1024:2048]), "sV")

        ln_state = {0: 0, 1: 0}

        def emit_ln_tiles(tiles):
            nT = len(tiles)
            stat_v = {}
            sln = {}
            out_v = []

            def phase1(i):
                p = i % 2
                xsl = tiles[i][0]
                E.wait(v, "sV", ln_state[p])
                vv = E.inc(v.bn_stats(bnb[:, p * 14:p * 14 + 6], xsl[:, 0:512]), "sV")
                E.wait(v, "sV", vv)
                vv = E.inc(v.bn_stats(bnb[:, p * 14 + 6:p * 14 + 12],
                                      xsl[:, 512:1024]), "sV")
                E.wait(v, "sV", vv)
                vv = E.inc(v.bn_aggr(bnb[:, p * 14 + 12:p * 14 + 14],
                                     bnb[:, p * 14:p * 14 + 12]), "sV")
                E.wait(v, "sV", vv)
                vv = E.inc(v.tensor_scalar_add(cbl[:, 4 * p:4 * p + 1],
                                               bnb[:, p * 14 + 13:p * 14 + 14],
                                               EPS), "sV")
                stat_v[i] = vv

            def phase2(i):
                p = i % 2
                E.wait(s, "sV", stat_v[i])
                ss_ = E.inc(s.activation(cbl[:, 4 * p + 1:4 * p + 2],
                                         cbl[:, 4 * p:4 * p + 1], ACT.Ln), "sS")
                E.wait(s, "sS", ss_)
                sln[i] = E.inc(s.activation(cbl[:, 4 * p + 2:4 * p + 3],
                                            cbl[:, 4 * p + 1:4 * p + 2],
                                            ACT.Exp, scale=-0.5), "sS")

            def phase3(i):
                p = i % 2
                xsl, outsl = tiles[i]
                E.wait(v, "sS", sln[i])
                vv = E.inc(v.scalar_tensor_tensor(cbl[:, 4 * p + 3:4 * p + 4],
                                                  bnb[:, p * 14 + 12:p * 14 + 13],
                                                  -1.0, cbl[:, 4 * p + 2:4 * p + 3],
                                                  ALU.mult, ALU.mult), "sV")
                E.wait(v, "sV", vv)
                vv = E.inc(v.tensor_scalar(outsl, xsl, cbl[:, 4 * p + 2:4 * p + 3],
                                           cbl[:, 4 * p + 3:4 * p + 4],
                                           ALU.mult, ALU.add), "sV")
                ln_state[p] = vv
                out_v.append(vv)

            phase1(0)
            phase2(0)
            for i in range(nT):
                if i + 1 < nT:
                    phase1(i + 1)
                    phase2(i + 1)
                phase3(i)
            return out_v

        def emit_ln_tile(xsl, outsl):
            return emit_ln_tiles([(xsl, outsl)])[0]

        def emit_ln_tr(tiles, dst, dst_off_fn, dst_stride, src, pre=None):
            nT = len(tiles)
            with nc.psum_tensor(f"ptr{scope_i[0]}", [128, 2048], BF16) as ptr:
                scope_i[0] += 1
                stat_v = {}
                sln = {}
                ap_v = {}
                tr_t = {}
                ev_v = {}

                def phase1(i):
                    p = i % 2
                    xsl = tiles[i][0]
                    if pre is not None:
                        pre(i)
                    E.wait(v, "sV", ln_state[p])
                    vv = E.inc(v.bn_stats(bnb[:, p * 14:p * 14 + 6],
                                          xsl[:, 0:512]), "sV")
                    E.wait(v, "sV", vv)
                    vv = E.inc(v.bn_stats(bnb[:, p * 14 + 6:p * 14 + 12],
                                          xsl[:, 512:1024]), "sV")
                    E.wait(v, "sV", vv)
                    vv = E.inc(v.bn_aggr(bnb[:, p * 14 + 12:p * 14 + 14],
                                         bnb[:, p * 14:p * 14 + 12]), "sV")
                    E.wait(v, "sV", vv)
                    vv = E.inc(v.tensor_scalar_add(cbl[:, 4 * p:4 * p + 1],
                                                   bnb[:, p * 14 + 13:p * 14 + 14],
                                                   EPS), "sV")
                    stat_v[i] = vv

                def phase2(i):
                    p = i % 2
                    E.wait(s, "sV", stat_v[i])
                    ss_ = E.inc(s.activation(cbl[:, 4 * p + 1:4 * p + 2],
                                             cbl[:, 4 * p:4 * p + 1], ACT.Ln), "sS")
                    E.wait(s, "sS", ss_)
                    sln[i] = E.inc(s.activation(cbl[:, 4 * p + 2:4 * p + 3],
                                                cbl[:, 4 * p + 1:4 * p + 2],
                                                ACT.Exp, scale=-0.5), "sS")

                def phase3(i):
                    p = i % 2
                    xsl, outsl = tiles[i]
                    E.wait(v, "sS", sln[i])
                    vv = E.inc(v.scalar_tensor_tensor(
                        cbl[:, 4 * p + 3:4 * p + 4],
                        bnb[:, p * 14 + 12:p * 14 + 13],
                        -1.0, cbl[:, 4 * p + 2:4 * p + 3],
                        ALU.mult, ALU.mult), "sV")
                    E.wait(v, "sV", vv)
                    vv = E.inc(v.tensor_scalar(outsl, xsl,
                                               cbl[:, 4 * p + 2:4 * p + 3],
                                               cbl[:, 4 * p + 3:4 * p + 4],
                                               ALU.mult, ALU.add), "sV")
                    ln_state[p] = vv
                    ap_v[i] = vv

                def trans(i):
                    p = i % 2
                    E.wait(t, "sV", ap_v[i])
                    E.wait(t, "sV", ev_v[i - 2] if i >= 2 else last_ptr_evac[0])
                    for j in range(8):
                        _mm2 = t.transpose(
                            ptr[:, p * 1024 + j * 128: p * 1024 + (j + 1) * 128],
                            src[:, i * 1024 + j * 128: i * 1024 + (j + 1) * 128],
                            identb[:, :])
                    tr_t[i] = E.inc(_mm2, "sT")

                def evac(i):
                    p = i % 2
                    E.wait(v, "sT", tr_t[i])
                    ev_v[i] = E.inc(v.tensor_copy(
                        bass.AP(dst, dst_off_fn(i),
                                [[dst.shape[1], 128], [dst_stride, 8], [1, 128]]),
                        ptr[:, p * 1024: p * 1024 + 1024]), "sV")

                phase1(0)
                phase2(0)
                for i in range(nT):
                    if i + 1 < nT:
                        phase1(i + 1)
                        phase2(i + 1)
                    phase3(i)
                    trans(i)
                    evac(i)
                last_ptr_evac[0] = ev_v[nT - 1]
            return ev_v[nT - 1]

        last_ptr_evac = [0]

        def emit_transposes(src, src_col, dst, dst_off, dst_stride, n=8,
                            src_v=None):
            with nc.psum_tensor(f"ptr{scope_i[0]}", [128, 1024], BF16) as ptr:
                scope_i[0] += 1
                if src_v is not None:
                    E.wait(t, "sV", src_v)
                E.wait(t, "sV", last_ptr_evac[0])
                for j in range(n):
                    _mm = t.transpose(ptr[:, j * 128:(j + 1) * 128],
                                src[:, src_col + j * 128: src_col + (j + 1) * 128],
                                identb[:, :])
                tt = E.inc(_mm, "sT")
                E.wait(v, "sT", tt)
                vv = E.inc(v.tensor_copy(
                    bass.AP(dst, dst_off,
                            [[dst.shape[1], 128], [dst_stride, n], [1, 128]]),
                    ptr[:, 0:n * 128]), "sV")
                last_ptr_evac[0] = vv
            return vv

        _kstage = int(os.environ.get("KSTAGE", "99"))
        if _kstage == 0:
            E.wait(sp, "dW")
            E.wait(sp, "dC")
            return nc

        n_groups = kv_len // 512
        grp_xfree = [0] * n_groups
        prev_tr_T = 0
        prev_kevac = 0
        prev_vevac = 0
        for gi in range(n_groups):
            par = gi % 2
            dxn = "dX0" if par == 0 else "dX1"
            if gi >= 2:
                E.wait(sp, "sV", grp_xfree[gi - 2])
            ld_t = []
            for ti in range(4):
                sem = f"dP{par}T{ti}"
                ld_t.append((sem, E.inc(sp.dma_start(
                    bass.AP(xb[par], ti * 1024, [[4096, 128], [1, 1024]]),
                    bass.AP(xkv_d, (gi * 512 + ti * 128) * D,
                            [[1024, 128], [1, 1024]])),
                    sem, 16)))
            E.wait(v, "sT", prev_tr_T)
            tr_done = emit_ln_tr(
                [(xb[par][:, ti * 1024:(ti + 1) * 1024],
                  sc8a[:, ti * 1024:(ti + 1) * 1024]) for ti in range(4)],
                fmA, lambda i: i * 128, 512, sc8a,
                pre=lambda i, ld_t=ld_t: E.wait(v, ld_t[i][0], ld_t[i][1]))
            prev_tr_T = E.c["sT"]
            grp_xfree[gi] = E.c["sV"]
            E.wait(t, "sV", tr_done)
            E.wait(t, "sS", prev_kevac)
            for dk2 in range(2):
                for cc in range(8):
                    _mm = t.matmul(ps_s[:, dk2 * 512:(dk2 + 1) * 512],
                             wk[:, cc * 256 + dk2 * 128: cc * 256 + (dk2 + 1) * 128],
                             fmA[:, cc * 512:(cc + 1) * 512],
                             start=(cc == 0), stop=(cc == 7))
            tk = E.inc(_mm, "sT")
            E.wait(s, "sT", tk)
            for dk2 in range(2):
                prev_kevac = E.inc(s.activation(
                    kfm[:, dk2 * kv_len + gi * 512: dk2 * kv_len + (gi + 1) * 512],
                    ps_s[:, dk2 * 512:(dk2 + 1) * 512], ACT.Identity,
                    bias=bkc[:, dk2:dk2 + 1], scale=1.0), "sS")
            E.wait(t, "sS", prev_vevac)
            for ti in range(4):
                for cc in range(8):
                    _mm = t.matmul(ps_o[:, ti * 256:(ti + 1) * 256],
                             fmA[:, cc * 512 + ti * 128: cc * 512 + (ti + 1) * 128],
                             wv[:, cc * 256:(cc + 1) * 256],
                             start=(cc == 0), stop=(cc == 7))
            tv_ = E.inc(_mm, "sT")
            E.wait(s, "sT", tv_)
            if gi == 0:
                E.wait(s, "sG", gset)
            for ti in range(4):
                tile = gi * 4 + ti
                E.inc(s.activation(
                    bass.AP(vaug, 32 + tile * VA + 32,
                            [[vw, 128], [161, 2], [1, 64]]),
                    bass.AP(ps_o, ti * 256 + 64,
                            [[1024, 128], [128, 2], [1, 64]]),
                    ACT.Identity, scale=1.0), "sS")
                prev_vevac = E.inc(s.activation(
                    bass.AP(vaug, 32 + tile * VA + 96,
                            [[vw, 128], [161, 2], [1, 64]]),
                    bass.AP(ps_o, ti * 256,
                            [[1024, 128], [128, 2], [1, 64]]),
                    ACT.Identity, scale=1.0), "sS")

        def emit_pass2(cfgs):
            if _kstage == 1:
                E.wait(sp, "sV")
                E.wait(sp, "sS")
                return nc

            for eng in (t, v, s):
                E.wait(eng, "sT")
                E.wait(eng, "sV")
                E.wait(eng, "sS")

            E.wait(sp, "sV")
            exp_done = {}
            pv_done = {}
            n_exp = [0]
            oslot_free = {0: 0, 1: 0}
            stg_guard = {0: 0, 1: 0}
            zrow_guard = {0: 0, 1: 0}
            last_s3 = [0]
            xn3f_guard = [0]
            chunk_out_dma = {0: 0, 1: 0}
            chunk_xfree = {}
            prev_wo_T = 0
            prev_sc8a_T = 0
            prev_sums_T = 0
            prev_gate_V = 0
            prev_xmix_T = {"r": 0, "k": 0}
            prev_fmB_T = 0
            prev_x1_V = 0

            if _kstage < 90:
                cfgs = cfgs[:max(0, _kstage - 1)]
            qst_guard_t = [0]
            qst_guard_v = [0]
            chunk_ld = {}

            def emit_load(ci):
                ch_ = cfgs[ci]
                nt_ = ch_["W"] // 128
                par_ = ci % 2
                dxn_ = "dX0" if par_ == 0 else "dX1"
                don_ = "dO0" if par_ == 0 else "dO1"
                E.wait(sp, don_, chunk_out_dma[par_])
                if ci >= 2:
                    E.wait(sp, "sV", chunk_xfree[ci - 2])
                chunk_ld[ci] = E.inc(sp.dma_start(
                    bass.AP(xb[par_], 0, [[4096, 128], [1024, nt_], [1, 1024]]),
                    bass.AP(xkv_d, ch_["qs"] * D,
                            [[1024, 128], [131072, nt_], [1, 1024]])),
                    dxn_, 16)

            emit_load(0)
            for ci, ch in enumerate(cfgs):
                W = ch["W"]
                nt = W // 128
                par = ci % 2
                dxn = "dX0" if par == 0 else "dX1"
                don = "dO0" if par == 0 else "dO1"
                kbmax = (ch["qs"] + W) // 128
                dq0 = ch["qs"] // 128
                ld = chunk_ld[ci]

                E.wait(v, dxn, ld)
                E.wait(v, "sT", prev_sc8a_T)
                E.wait(v, "sT", prev_xmix_T["k"])
                tr_done = emit_ln_tr(
                    [(xb[par][:, ti * 1024:(ti + 1) * 1024],
                      sc8a[:, ti * 1024:(ti + 1) * 1024]) for ti in range(nt)],
                    fmA, lambda i: i * 128, 512, sc8a)

                E.wait(t, "sV", tr_done)
                E.wait(t, "sS")
                E.wait(t, "sV")
                E.wait(t, "sT", prev_fmB_T)
                slot_evac = {}
                for qcc in range(8):
                    slot = qcc % 4
                    if qcc >= 4:
                        E.wait(t, "sS", slot_evac[slot])
                    for cc in range(8):
                        _mm = t.matmul(ps_s[:, slot * 512: slot * 512 + W],
                                 wq[:, cc * 1024 + qcc * 128: cc * 1024 + (qcc + 1) * 128],
                                 fmA[:, cc * 512: cc * 512 + W],
                                 start=(cc == 0), stop=(cc == 7))
                    tq = E.inc(_mm, "sT")
                    E.wait(s, "sT", tq)
                    slot_evac[slot] = E.inc(s.activation(
                        fmB[:, qcc * 512: qcc * 512 + W],
                        ps_s[:, slot * 512: slot * 512 + W], ACT.Identity,
                        bias=bqc[:, qcc:qcc + 1], scale=1.0), "sS")
                qevac_all = E.c["sS"]
                if ci + 1 < len(cfgs):
                    emit_load(ci + 1)

                with nc.psum_tensor(f"pso2_{scope_i[0]}", [128, 1024],
                                    F32) as ps_o2:
                    scope_i[0] += 1
                    evac_q = []

                    def obank(it_):
                        return ps_o if it_ % 2 == 0 else ps_o2

                    def evac_phase_a(p):
                        gi_ = p["gi"]
                        zrow = rzrow if gi_ == 0 else rirow
                        ob_ = p["ob"]
                        E.wait(s, "sT", p["lastpv"])
                        E.wait(s, "sS", last_s3[0])
                        E.wait(s, "sV", prev_x1_V)
                        E.wait(s, "sT", prev_wo_T)
                        E.wait(s, "sT", zrow_guard[gi_])
                        s1 = E.inc(s.activation(
                            rwsc[0:1, 0:W],
                            ob_[p["zp"]:p["zp"] + 1, p["ocol"]: p["ocol"] + W],
                            ACT.Identity, scale=1.0), "sS")
                        E.wait(s, "sS", s1)
                        s2 = E.inc(s.activation(
                            rwsc[0:1, 512:512 + W], rwsc[0:1, 0:W],
                            ACT.Ln), "sS")
                        E.wait(s, "sS", s2)
                        with nc.allow_low_precision(
                                reason="1/Z row in bf16: P is bf16 anyway"):
                            s3 = E.inc(s.activation(
                                zrow[0:1, 0:W], rwsc[0:1, 512:512 + W],
                                ACT.Exp, scale=-1.0), "sS")
                        p["s3"] = s3
                        last_s3[0] = s3

                    def evac_phase_b(p):
                        gi_ = p["gi"]
                        zrow = rzrow if gi_ == 0 else rirow
                        stg = rzbc if gi_ == 0 else o2s
                        ob_ = p["ob"]
                        rb = p["rowb"]
                        oc = p["ocol"]
                        cc2_ = p["cc2"]
                        E.wait(v, "sT", p["lastpv"])
                        E.wait(v, "sV", stg_guard[gi_])
                        if gi_ == 1:
                            E.wait(v, "sT", prev_sums_T)
                        with nc.allow_low_precision(
                                reason="O_raw staged bf16; O is bf16 anyway"):
                            c1 = E.inc(v.tensor_copy(
                                stg[rb:rb + 64, 0:W],
                                ob_[rb:rb + 64, oc: oc + W]), "sV")
                        E.wait(t, "sS", p["s3"])
                        E.wait(t, "sV", c1)
                        r1 = E.inc(t.matmul(
                            ob_[0:128, oc: oc + W],
                            onesr[0:1, :],
                            zrow[0:1, 0:W],
                            start=True, stop=True), "sT")
                        zrow_guard[gi_] = r1
                        E.wait(v, "sT", r1)
                        vv = E.inc(v.tensor_mul(
                            fmA[rb:rb + 64, cc2_ * 512: cc2_ * 512 + W],
                            stg[rb:rb + 64, 0:W],
                            ob_[rb:rb + 64, oc: oc + W]), "sV")
                        stg_guard[gi_] = vv
                        oslot_free[p["it"] % 2] = vv
                        E.wait(s, "sV", vv)
                        E.inc(s.activation(
                            fmA[rb:rb + 64, cc2_ * 512: cc2_ * 512 + W],
                            fmA[rb:rb + 64, cc2_ * 512: cc2_ * 512 + W],
                            ACT.Identity,
                            bias=bvc[rb:rb + 64, p["h"] // 2: p["h"] // 2 + 1],
                            scale=1.0), "sS")

                    E.wait(t, "sS", qevac_all)
                    jobs = []
                    itmeta = {}
                    for it2 in range(8):
                        h = it2 // 2
                        gp = it2 % 2
                        qh0 = 8 * gp + h
                        qh1 = 8 * gp + 4 + h
                        gsz = max(1, 1024 // (2 * W))
                        groups = []
                        kb = 0
                        while kb < kbmax:
                            kbs = list(range(kb, min(kb + gsz, kbmax)))
                            groups.append(kbs)
                            kb += len(kbs)
                        itmeta[it2] = dict(h=h, gq=(2 * gp, 2 * gp + 1),
                                           qcc=(qh0 // 2, qh1 // 2),
                                           rowb=64 * (h % 2),
                                           zp=64 if h % 2 == 0 else 32,
                                           ng=len(groups))
                        for ln_, kbs in enumerate(groups):
                            jobs.append((it2, ln_, kbs))
                    base0 = n_exp[0]
                    qkt_t = {}

                    it_lastq = {}
                    qcp_v = {}

                    def qslot(it_, gi):
                        if it_ % 2 == 0:
                            return qstage, gi * 512
                        return sc8a, 3072 + gi * 512

                    def emit_qcopy(it_):
                        m = itmeta[it_]
                        orow = 64 - m["rowb"]
                        E.wait(v, "sS", qevac_all)
                        E.wait(v, "sT", qst_guard_t[0])
                        E.wait(v, "sV", qst_guard_v[0])
                        if it_ >= 2:
                            E.wait(v, "sT", it_lastq.get(it_ - 2, 0))
                        for gi in (0, 1):
                            qb, c0 = qslot(it_, gi)
                            E.inc(v.memset(qb[orow:orow + 64, c0:c0 + W],
                                           0.0), "sV")
                            qcp_v[it_] = E.inc(v.tensor_copy(
                                qb[m["rowb"]:m["rowb"] + 64, c0:c0 + W],
                                fmB[m["rowb"]:m["rowb"] + 64,
                                    m["qcc"][gi] * 512: m["qcc"][gi] * 512 + W]),
                                "sV")

                    def emit_qkt_job(k):
                        it_, ln2, kbs_ = jobs[k]
                        m = itmeta[it_]
                        n = base0 + k
                        sp2 = n % 2
                        if ln2 == 0:
                            emit_qcopy(it_)
                            E.wait(t, "sV", qcp_v[it_])
                        if n >= 2:
                            E.wait(t, "sS", exp_done[n - 2])
                        for j, kbj in enumerate(kbs_):
                            stat = kfm[:, (m["h"] // 2) * kv_len + kbj * 128:
                                       (m["h"] // 2) * kv_len + (kbj + 1) * 128]
                            for gi in (0, 1):
                                qb, c0 = qslot(it_, gi)
                                _mm = t.matmul(
                                    ps_s[:, sp2 * 1024 + (2 * j + gi) * W:
                                         sp2 * 1024 + (2 * j + gi + 1) * W],
                                    stat,
                                    qb[:, c0: c0 + W],
                                    start=True, stop=True)
                        qkt_t[n] = E.inc(_mm, "sT")
                        it_lastq[it_] = qkt_t[n]

                    def ptsl(n2):
                        sl = n2 % 3
                        if sl == 0:
                            return pt, 0, 2048
                        if sl == 1:
                            return pt, 1024, 2048
                        return xn3f, 0, 4104

                    needs = {}

                    def emit_pv(k2, bk):
                        it_, ln2, kbs_ = jobs[k2]
                        m2 = itmeta[it_]
                        n2 = base0 + k2
                        pb, pbase, _pp = ptsl(n2)
                        ob_ = obank(it_)
                        if ln2 == 0 and it_ >= 2:
                            while evac_q:
                                _sl, fn_, rec_ = evac_q.pop(0)
                                fn_(rec_)
                            E.wait(t, "sV", oslot_free[it_ % 2])
                        E.wait(t, "sS", exp_done[n2])
                        nv, ng2 = needs[n2]
                        if nv:
                            E.wait(t, "sV", nv)
                        if ng2:
                            E.wait(t, "sG", ng2)
                        for j, kbj in enumerate(kbs_):
                            sbase = 32 + kbj * VA + (m2["h"] // 2) * 161
                            if m2["h"] % 2 == 0:
                                stat = vaug[:, sbase + 96: sbase + 161]
                                np_ = 65
                            else:
                                stat = vaug[:, sbase - 32: sbase + 96]
                                np_ = 128
                            for gi in (0, 1):
                                _mm = t.matmul(
                                    ob_[0:np_, gi * 512: gi * 512 + W],
                                    stat,
                                    pb[:, pbase + (2 * j + gi) * W:
                                       pbase + (2 * j + gi + 1) * W],
                                    start=(ln2 == 0 and j == 0),
                                    stop=(kbj == kbmax - 1))
                        pv_done[n2] = E.inc(_mm, "sT")
                        if ln2 == m2["ng"] - 1:
                            recs = []
                            for gi in (0, 1):
                                recs.append(dict(
                                    h=m2["h"], gi=gi, it=it_, ob=ob_,
                                    ocol=gi * 512,
                                    zp=m2["zp"], rowb=m2["rowb"],
                                    cc2=2 * m2["gq"][gi] + m2["h"] // 2,
                                    lastpv=pv_done[n2]))
                            evac_q.extend([
                                (bk + 1, evac_phase_a, recs[0]),
                                (bk + 3, evac_phase_a, recs[1]),
                                (bk + 5, evac_phase_b, recs[0]),
                                (bk + 7, evac_phase_b, recs[1])])

                    emit_qkt_job(0)
                    for k, (it2, ln_, kbs) in enumerate(jobs):
                        m = itmeta[it2]
                        n = base0 + k
                        sp2 = n % 2
                        pb, pbase, ppitch = ptsl(n)
                        if k + 1 < len(jobs):
                            emit_qkt_job(k + 1)
                        E.wait(s, "sT", qkt_t[n])
                        if n >= 3:
                            E.wait(s, "sT", pv_done[n - 3])
                        if n % 3 == 2:
                            E.wait(s, "sV", xn3f_guard[0])
                        exp_done[n] = E.inc(s.activation(
                            pb[:, pbase: pbase + 2 * len(kbs) * W],
                            ps_s[:, sp2 * 1024: sp2 * 1024 + 2 * len(kbs) * W],
                            ACT.Exp, bias=nexpc[:, 0:1], scale=0.125), "sS")
                        need_v = 0
                        need_g = 0
                        for j, kbj in enumerate(kbs):
                            if kbj >= dq0:
                                d = kbj * 128 - ch["qs"]
                                base = pbase + 2 * j * W
                                E.wait(v, "sS", exp_done[n])
                                for gi in (0, 1):
                                    need_v = E.inc(v.tensor_mul(
                                        pb[:, base + gi * W + d:
                                           base + gi * W + d + 128],
                                        pb[:, base + gi * W + d:
                                           base + gi * W + d + 128],
                                        trim[:, :]), "sV")
                                if d > 0:
                                    E.wait(g, "sS", exp_done[n])
                                    need_g = E.inc(g.memset(
                                        bass.AP(pb, base,
                                                [[ppitch, 128], [W, 2], [1, d]]),
                                        0.0), "sG")
                        needs[n] = (need_v, need_g)
                        if k >= 2:
                            emit_pv(k - 2, k)
                        while evac_q and evac_q[0][0] <= k:
                            _sl, fn_, rec_ = evac_q.pop(0)
                            fn_(rec_)
                    for k2_ in range(max(0, len(jobs) - 2), len(jobs)):
                        emit_pv(k2_, len(jobs))
                    n_exp[0] = base0 + len(jobs)
                    while evac_q:
                        _sl, fn_, rec_ = evac_q.pop(0)
                        fn_(rec_)
                attn_S = E.c["sS"]
                prev_fmB_T = E.c["sT"]

                with nc.psum_tensor(f"psst{scope_i[0]}", [33, 512], F32) as psst, \
                     nc.psum_tensor(f"pstc{scope_i[0]}", [128, 16], BF16) as pstc:
                    scope_i[0] += 1
                    sq_s = {}
                    mm_t = {}
                    E.wait(s, "sS", attn_S)
                    E.wait(s, "sT", prev_sums_T)
                    E.wait(s, "sV", prev_gate_V)
                    E.wait(t, "sS", attn_S)
                    for cc2 in range(8):
                        if cc2 >= 1:
                            E.wait(s, "sT", mm_t[cc2 - 1])
                        sq_s[cc2] = E.inc(s.activation(
                            o2s[:, 0:W],
                            fmA[:, cc2 * 512: cc2 * 512 + W], ACT.Square), "sS")
                        _mm = t.matmul(psst[0:33, 0:W], ones2[:, 0:33],
                                 fmA[:, cc2 * 512: cc2 * 512 + W],
                                 start=(cc2 == 0), stop=False)
                        E.wait(t, "sS", sq_s[cc2])
                        _mm = t.matmul(psst[0:33, 0:W], ones2[:, 33:66],
                                 o2s[:, 0:W],
                                 start=False, stop=(cc2 == 7))
                        mm_t[cc2] = E.inc(_mm, "sT")
                    tst = mm_t[7]
                    prev_sums_T = tst
                    E.wait(v, "sT", tst)
                    vv = E.inc(v.tensor_scalar_mul(rzrow[0:1, 0:W], psst[0:1, 0:W],
                                                   -1.0 / D), "sV")
                    E.wait(v, "sV", vv)
                    vv = E.inc(v.tensor_scalar_mul(rwsc[0:1, 0:W], psst[0:1, 0:W],
                                                   1.0 / D), "sV")
                    E.wait(v, "sV", vv)
                    vv = E.inc(v.tensor_mul(rwsc[0:1, 512:512 + W],
                                            rwsc[0:1, 0:W], rwsc[0:1, 0:W]), "sV")
                    E.wait(v, "sV", vv)
                    vv = E.inc(v.scalar_tensor_tensor(
                        rwsc[0:1, 0:W], psst[32:33, 0:W], 1.0 / D,
                        rwsc[0:1, 512:512 + W], ALU.mult, ALU.subtract), "sV")
                    E.wait(v, "sV", vv)
                    vv = E.inc(v.tensor_scalar_add(rwsc[0:1, 0:W],
                                                   rwsc[0:1, 0:W], EPS), "sV")
                    E.wait(s, "sV", vv)
                    ssl = E.inc(s.activation(rwsc[0:1, 512:512 + W],
                                             rwsc[0:1, 0:W], ACT.Ln), "sS")
                    E.wait(s, "sS", ssl)
                    ri_s = E.inc(s.activation(rirow[0:1, 0:W], rwsc[0:1, 512:512 + W],
                                              ACT.Exp, scale=0.5), "sS")
                    E.wait(v, "sT", tst)
                    vv0 = E.inc(v.tensor_copy(qstage[0:1, 512:512 + W], psst[0:1, 0:W]), "sV")
                    E.wait(v, "sV", vv0)
                    vsf = E.inc(v.tensor_copy(qstage[32:33, 512:512 + W], psst[32:33, 0:W]), "sV")
                    E.wait(t, "sV", vsf)
                    for ti in range(nt):
                        _mm = t.transpose(pstc[:, ti * 4:ti * 4 + 1],
                                    qstage[0:1, 512 + ti * 128:512 + (ti + 1) * 128], identb[0:1, 0:1])
                        _mm = t.transpose(pstc[:, ti * 4 + 2:ti * 4 + 3],
                                    qstage[32:33, 512 + ti * 128:512 + (ti + 1) * 128],
                                    identb[32:33, 32:33])
                    ttc = E.inc(_mm, "sT")
                    qst_guard_t[0] = ttc
                    rr_s = {}
                    ln_s = {}
                    for ti in range(nt):
                        E.wait(v, "sT", ttc)
                        if ti >= 1:
                            E.wait(v, "sS", ln_s[ti - 1])
                        vv = E.inc(v.tensor_scalar_mul(cbl[:, 8:9],
                                                       pstc[:, ti * 4:ti * 4 + 1],
                                                       1.0 / D), "sV")
                        E.wait(v, "sV", vv)
                        vv = E.inc(v.tensor_mul(cbl[:, 9:10],
                                                cbl[:, 8:9], cbl[:, 8:9]), "sV")
                        E.wait(v, "sV", vv)
                        vv = E.inc(v.scalar_tensor_tensor(
                            cbl[:, 10:11], pstc[:, ti * 4 + 2:ti * 4 + 3], 1.0 / D,
                            cbl[:, 9:10], ALU.mult, ALU.subtract), "sV")
                        E.wait(v, "sV", vv)
                        vv = E.inc(v.tensor_scalar_add(cbl[:, 10:11],
                                                       cbl[:, 10:11], EPS), "sV")
                        E.wait(s, "sV", vv)
                        if ti >= 1:
                            E.wait(s, "sS", rr_s[ti - 1])
                        ln_s[ti] = E.inc(s.activation(cbl[:, 11:12], cbl[:, 10:11],
                                                      ACT.Ln), "sS")
                        E.wait(s, "sS", ln_s[ti])
                        rr_s[ti] = E.inc(s.activation(cbl[:, 12 + ti:13 + ti],
                                                      cbl[:, 11:12],
                                                      ACT.Exp, scale=-0.5), "sS")
                    vmu_done = E.c["sV"]
                    x1_done = {}
                    for ti in range(nt):
                        wslot = ti % 2
                        if ti >= 2:
                            E.wait(t, "sV", x1_done[ti - 2])
                        if ti == 0:
                            E.wait(t, "sS", attn_S)
                        for hf in range(2):
                            pss = ps_s[:, wslot * 1024 + hf * 512: wslot * 1024 + (hf + 1) * 512]
                            for cc2 in range(8):
                                _mm = t.matmul(pss,
                                         fmA[:, cc2 * 512 + ti * 128: cc2 * 512 + (ti + 1) * 128],
                                         wo[:, cc2 * 1024 + hf * 512: cc2 * 1024 + (hf + 1) * 512],
                                         start=(cc2 == 0), stop=False)
                            E.wait(t, "sV", vmu_done)
                            E.wait(t, "sS", ri_s)
                            _mm = t.matmul(pss, rzrow[0:1, ti * 128:(ti + 1) * 128],
                                     cwrow[0:1, hf * 512:(hf + 1) * 512],
                                     start=False, stop=False)
                            _mm = t.matmul(pss, rirow[0:1, ti * 128:(ti + 1) * 128],
                                     obbf[0:1, hf * 512:(hf + 1) * 512],
                                     start=False, stop=True)
                        two = E.inc(_mm, "sT")
                        E.wait(v, "sT", two)
                        E.wait(v, "sS", rr_s[ti])
                        x1_done[ti] = E.inc(v.scalar_tensor_tensor(
                            xb[par][:, ti * 1024:(ti + 1) * 1024],
                            ps_s[:, wslot * 1024:(wslot + 1) * 1024],
                            cbl[:, 12 + ti:13 + ti],
                            xb[par][:, ti * 1024:(ti + 1) * 1024],
                            ALU.mult, ALU.add), "sV")
                    prev_wo_T = E.c["sT"]
                    prev_x1_V = E.c["sV"]

                E.wait(v, "sV", prev_x1_V)
                E.wait(v, "sT", prev_fmB_T)
                E.wait(v, "sS", attn_S)
                if not ch["ghost"]:
                    ssl_ = TAIL_SLOT[ch["tail_src"]]
                    vv = E.inc(v.tensor_copy(
                        bass.AP(xn3f, 0, [[4104, 128], [513, 8], [1, 1]]),
                        bass.AP(tails, ssl_ * 8, [[24, 128], [1, 8]])), "sV")
                    tails_in_v = vv
                tr3 = emit_ln_tr(
                    [(xb[par][:, ti * 1024:(ti + 1) * 1024],
                      sc8a[:, ti * 1024:(ti + 1) * 1024]) for ti in range(nt)],
                    xn3f, lambda i: 1 + i * 128, 513, sc8a)
                prev_sc8a_T = E.c["sT"]
                chunk_xfree[ci] = E.c["sV"]
                if ch["tail_dst"] is not None:
                    dsl_ = TAIL_SLOT[ch["tail_dst"]]
                    E.wait(v, "sV", tr3)
                    E.inc(v.tensor_copy(
                        bass.AP(tails, dsl_ * 8, [[24, 128], [1, 8]]),
                        bass.AP(xn3f, W, [[4104, 128], [513, 8], [1, 1]])), "sV")
                xn3f_guard[0] = E.c["sV"]
                if ch["ghost"]:
                    continue

                def emit_mix(mu, mu2, dstbuf=None):
                    dstbuf = xmix if dstbuf is None else dstbuf
                    last = 0
                    for cc in range(8):
                        vv = E.inc(v.tensor_scalar_mul(
                            dstbuf[:, cc * 512: cc * 512 + W],
                            xn3f[:, cc * 513: cc * 513 + W],
                            mu2[:, cc:cc + 1]), "sV")
                        E.wait(v, "sV", vv)
                        last = E.inc(v.scalar_tensor_tensor(
                            dstbuf[:, cc * 512: cc * 512 + W],
                            xn3f[:, cc * 513 + 1: cc * 513 + 1 + W],
                            mu[:, cc:cc + 1],
                            dstbuf[:, cc * 512: cc * 512 + W],
                            ALU.mult, ALU.add), "sV")
                    return last

                E.wait(v, "sV", tr3)
                E.wait(v, "sV", tails_in_v)
                E.wait(v, "sT", prev_xmix_T["k"])
                mix_r = emit_mix(murc, murc2)
                E.wait(t, "sV", mix_r)
                E.wait(t, "sV", prev_x1_V)
                sig_s = {}
                for ti in range(nt):
                    wslot = ti % 2
                    if ti >= 2:
                        E.wait(t, "sS", sig_s[ti - 2])
                    for hf in range(2):
                        pss = ps_s[:, wslot * 1024 + hf * 512: wslot * 1024 + (hf + 1) * 512]
                        for cc in range(8):
                            _mm = t.matmul(pss,
                                     xmix[:, cc * 512 + ti * 128: cc * 512 + (ti + 1) * 128],
                                     wr[:, cc * 1024 + hf * 512: cc * 1024 + (hf + 1) * 512],
                                     start=(cc == 0), stop=(cc == 7))
                    tr_ = E.inc(_mm, "sT")
                    E.wait(s, "sT", tr_)
                    E.wait(s, "sT", prev_sc8a_T)
                    sig_s[ti] = E.inc(s.activation(
                        sc8a[:, ti * 1024:(ti + 1) * 1024],
                        ps_s[:, wslot * 1024:(wslot + 1) * 1024], ACT.Sigmoid), "sS")
                prev_xmix_T["r"] = E.c["sT"]
                E.wait(v, "sT", prev_wo_T)
                mix_k = emit_mix(mukc, mukc2, dstbuf=fmA)
                E.wait(t, "sV", mix_k)
                E.wait(t, "sS", sig_s[nt - 1])
                kact_v = {}
                for kcc in range(8):
                    slot = kcc % 4
                    if kcc >= 4:
                        E.wait(t, "sV", kact_v[kcc - 4])
                    for cc in range(8):
                        _mm = t.matmul(ps_s[:, slot * 512: slot * 512 + W],
                                 wkf[:, cc * 1024 + kcc * 128: cc * 1024 + (kcc + 1) * 128],
                                 fmA[:, cc * 512: cc * 512 + W],
                                 start=(cc == 0), stop=(cc == 7))
                    tk_ = E.inc(_mm, "sT")
                    E.wait(v, "sT", tk_)
                    if kcc >= 1:
                        E.wait(v, "sV", kact_v[kcc - 1])
                    vv = E.inc(v.tensor_scalar_max(
                        qstage[:, 0:W], ps_s[:, slot * 512: slot * 512 + W], 0.0), "sV")
                    E.wait(v, "sV", vv)
                    kact_v[kcc] = E.inc(v.tensor_mul(
                        fmB[:, kcc * 512: kcc * 512 + W],
                        qstage[:, 0:W], qstage[:, 0:W]), "sV")
                prev_xmix_T["k"] = E.c["sT"]
                qst_guard_v[0] = E.c["sV"]
                gate_v = {}
                for ti in range(nt):
                    vslot = ti % 2
                    E.wait(t, "sV", kact_v[7])
                    if ti >= 2:
                        E.wait(t, "sV", gate_v[ti - 2])
                    for hf in range(2):
                        if vslot == 0:
                            pss = ps_o[:, hf * 512:(hf + 1) * 512]
                        else:
                            pss = ps_s[:, hf * 512:(hf + 1) * 512]
                        for cc in range(8):
                            _mm = t.matmul(pss,
                                     fmB[:, cc * 512 + ti * 128: cc * 512 + (ti + 1) * 128],
                                     wvf[:, cc * 1024 + hf * 512: cc * 1024 + (hf + 1) * 512],
                                     start=(cc == 0), stop=(cc == 7))
                    tvv = E.inc(_mm, "sT")
                    E.wait(v, "sT", tvv)
                    E.wait(v, "sS", sig_s[ti])
                    E.wait(v, "sT", prev_sums_T)
                    if ti >= 1:
                        E.wait(v, "sV", gate_v[ti - 1])
                    psrc = ps_o if vslot == 0 else ps_s
                    for hf in range(2):
                        vv = E.inc(v.tensor_mul(
                            o2s[:, 0:512],
                            sc8a[:, ti * 1024 + hf * 512: ti * 1024 + (hf + 1) * 512],
                            psrc[:, hf * 512:(hf + 1) * 512]), "sV")
                        E.wait(v, "sV", vv)
                        gate_v[ti] = E.inc(v.tensor_add(
                            xb[par][:, ti * 1024 + hf * 512: ti * 1024 + (hf + 1) * 512],
                            xb[par][:, ti * 1024 + hf * 512: ti * 1024 + (hf + 1) * 512],
                            o2s[:, 0:512]), "sV")
                        E.wait(v, "sV", gate_v[ti])
                prev_gate_V = E.c["sV"]
                chunk_xfree[ci] = E.c["sV"]
                qst_guard_v[0] = E.c["sV"]
                xn3f_guard[0] = E.c["sV"]

                E.wait(sp, "sV", prev_gate_V)
                chunk_out_dma[par] = E.inc(sp.dma_start(
                    bass.AP(out_d, ch["out_row"] * D,
                            [[1024, 128], [131072, nt], [1, 1024]]),
                    bass.AP(xb[par], 0, [[4096, 128], [1024, nt], [1, 1024]])),
                    don, 16)

            E.wait(sp, "dO0")
            E.wait(sp, "dO1")

        if role == "switch":
            engs = [t, v, s, g, sp]
            idx = [e.partition_id() % 2 for e in engs]
            snap = dict(E.c)
            snap_ptr = last_ptr_evac[0]
            snap_ln = dict(ln_state)
            for _ri in nc.Switch(engines=engs, index=idx, n=2):
                E.c.clear()
                E.c.update(snap)
                last_ptr_evac[0] = snap_ptr
                ln_state.clear()
                ln_state.update(snap_ln)
                emit_pass2(chunk_cfgs(_ri, seq_len)[0])
        else:
            emit_pass2(cfgs)

    return nc



def _pcol(vec):
    vec = np.asarray(vec, np.float32)
    return np.ascontiguousarray(vec.reshape(-1, 128).T)


def _consts():
    identb = np.eye(128, dtype=ml_dtypes.bfloat16)
    trim = (np.tril(np.ones((128, 128), np.float32)).T).astype(ml_dtypes.bfloat16)
    ones2 = np.zeros((128, 66), np.float32)
    ones2[:, 0] = 1.0
    ones2[:, 33 + 32] = 1.0
    ones2 = ones2.astype(ml_dtypes.bfloat16)
    onesr = np.ones((1, 128), ml_dtypes.bfloat16)
    onescol = np.ones((128, 1), ml_dtypes.bfloat16)
    return dict(identb=identb, trim=trim, ones2=ones2,
                onesr=onesr, onescol=onescol)


def kernel(L0_attn_x, L0_attn_Wq, L0_attn_bq, L0_attn_Wk, L0_attn_bk,
           L0_attn_Wv, L0_attn_bv, L0_attn_ln_g, L0_attn_ln_b, L0_attn_Wo,
           L0_attn_bo, L0_ffn_W_r, L0_ffn_W_k, L0_ffn_W_v, L0_ffn_mu_r,
           L0_ffn_mu_k):
    x = np.ascontiguousarray(np.asarray(L0_attn_x, np.float32))
    B, N, _ = x.shape
    shared = dict(
        wq=np.ascontiguousarray(L0_attn_Wq, np.float32),
        wk=np.ascontiguousarray(L0_attn_Wk, np.float32),
        wv=np.ascontiguousarray(L0_attn_Wv, np.float32),
        wo=np.ascontiguousarray(L0_attn_Wo, np.float32),
        wrT=np.ascontiguousarray(np.asarray(L0_ffn_W_r, np.float32).T),
        wkT=np.ascontiguousarray(np.asarray(L0_ffn_W_k, np.float32).T),
        wvT=np.ascontiguousarray(np.asarray(L0_ffn_W_v, np.float32).T),
        lng=_pcol(L0_attn_ln_g), lnb=_pcol(L0_attn_ln_b),
        bq=_pcol(L0_attn_bq), bk=_pcol(L0_attn_bk), bv=_pcol(L0_attn_bv),
        bo=np.ascontiguousarray(np.asarray(L0_attn_bo, np.float32).reshape(1, D)),
        mur=_pcol(L0_ffn_mu_r), muk=_pcol(L0_ffn_mu_k),
        **_consts(),
    )
    nc = build_nc(role="switch", seq_len=N)
    in_maps = []
    for c in range(8):
        b = c // 2
        m = dict(shared)
        m["xkv"] = x[b]
        in_maps.append(m)
    res = run_bass_kernel_spmd(nc, in_maps, core_ids=list(range(8)))
    out = np.empty((B, N, D), np.float32)
    for b in range(B):
        r0 = res.results[2 * b]["out"]
        r1 = res.results[2 * b + 1]["out"]
        out[b, 0:1024] = r0[0:1024]
        out[b, 3072:4096] = r0[1024:2048]
        out[b, 1024:3072] = r1[0:2048]
    return out


# version 36
# speedup vs baseline: 1.0094x; 1.0094x over previous
import contextlib
import os
import numpy as np
import ml_dtypes
import concourse.bass as bass
import concourse.mybir as mybir
from concourse.bass_utils import run_bass_kernel_spmd

F32 = mybir.dt.float32
BF16 = mybir.dt.bfloat16
ALU = mybir.AluOpType
ACT = mybir.ActivationFunctionType

D = 1024
EPS = 1e-5
EXPC = 20.0
VA = 322


class Emitter:
    def __init__(self, nc, es):
        self.h = {}
        self.c = {}
        for n in ("dW", "dC", "dX0", "dX1", "dO0", "dO1",
                  "dP0T0", "dP0T1", "dP0T2", "dP0T3",
                  "dP1T0", "dP1T1", "dP1T2", "dP1T3",
                  "sT", "sV", "sS", "sG"):
            self.h[n] = es.enter_context(nc.semaphore(n))
            self.c[n] = 0

    def inc(self, inst, name, k=1):
        inst.then_inc(self.h[name], k)
        self.c[name] += k
        return self.c[name]

    def wait(self, eng, name, val=None):
        v = self.c[name] if val is None else val
        if v > 0:
            eng.wait_ge(self.h[name], v)
        return v


def chunk_cfgs(role, seq_len):
    if role == "full":
        out = []
        for i in range(seq_len // 512):
            out.append(dict(qs=i * 512, W=512, ghost=False,
                            tail_src=("zero" if i == 0 else "prev"),
                            tail_dst="prev", out_row=i * 512))
        return out, seq_len
    if role == 0:
        return [
            dict(qs=2944, W=128, ghost=True, tail_src=None, tail_dst="g",
                 out_row=None),
            dict(qs=0, W=512, ghost=False, tail_src="zero", tail_dst="prev",
                 out_row=0),
            dict(qs=512, W=512, ghost=False, tail_src="prev", tail_dst="prev",
                 out_row=512),
            dict(qs=3072, W=512, ghost=False, tail_src="g", tail_dst="prev",
                 out_row=1024),
            dict(qs=3584, W=512, ghost=False, tail_src="prev", tail_dst="prev",
                 out_row=1536),
        ], 2048
    return [
        dict(qs=896, W=128, ghost=True, tail_src=None, tail_dst="g",
             out_row=None),
        dict(qs=1024, W=512, ghost=False, tail_src="g", tail_dst="prev",
             out_row=0),
        dict(qs=1536, W=512, ghost=False, tail_src="prev", tail_dst="prev",
             out_row=512),
        dict(qs=2048, W=512, ghost=False, tail_src="prev", tail_dst="prev",
             out_row=1024),
        dict(qs=2560, W=512, ghost=False, tail_src="prev", tail_dst="prev",
             out_row=1536),
    ], 2048


TAIL_SLOT = {"zero": 0, "prev": 1, "g": 2}


def build_nc(role="full", seq_len=4096):
    cfgs, out_rows = chunk_cfgs(role, seq_len)
    kv_len = seq_len
    nkvt = kv_len // 128

    nc = bass.Bass("TRN2", target_bir_lowering=False,
                   dynamic_dma_scratch_size=8192)
    dt = lambda n, s, d=F32, k="ExternalInput": nc.dram_tensor(n, s, d, kind=k)
    xkv_d = dt("xkv", [seq_len, D])
    wq_d = dt("wq", [D, D]); wk_d = dt("wk", [D, 256]); wv_d = dt("wv", [D, 256])
    wo_d = dt("wo", [D, D])
    wrT_d = dt("wrT", [D, D]); wkT_d = dt("wkT", [D, D]); wvT_d = dt("wvT", [D, D])
    lng_d = dt("lng", [128, 8]); lnb_d = dt("lnb", [128, 8])
    bq_d = dt("bq", [128, 8]); bk_d = dt("bk", [128, 2]); bv_d = dt("bv", [128, 2])
    bo_d = dt("bo", [1, D])
    mur_d = dt("mur", [128, 8]); muk_d = dt("muk", [128, 8])
    identb_d = dt("identb", [128, 128], BF16)
    trim_d = dt("trim", [128, 128], BF16)
    ones2_d = dt("ones2", [128, 66], BF16)
    onesr_d = dt("onesr", [1, 128], BF16)
    onescol_d = dt("onescol", [128, 1], BF16)
    out_d = dt("out", [out_rows, D], F32, k="ExternalOutput")

    es = contextlib.ExitStack()
    sb = lambda n, s, d: es.enter_context(nc.sbuf_tensor(n, s, d))
    wq = sb("wq_s", [128, 8192], BF16)
    wk = sb("wk_s", [128, 2048], BF16)
    wv = sb("wv_s", [128, 2048], BF16)
    wo = sb("wo_s", [128, 8192], BF16)
    wr = sb("wr_s", [128, 8192], BF16)
    wkf = sb("wkf_s", [128, 8192], BF16)
    wvf = sb("wvf_s", [128, 8192], BF16)
    kfm = sb("kfm", [128, 2 * kv_len], BF16)
    vaug = sb("vaug", [128, nkvt * VA + 32], BF16)
    xb = [sb("xb0", [128, 4096], F32), sb("xb1", [128, 4096], F32)]
    sc8a = sb("sc8a", [128, 4096], BF16)
    fmA = sb("fmA", [128, 4096], BF16)
    fmB = sb("fmB", [128, 4096], BF16)
    xn3f = sb("xn3f", [128, 8 * 513], BF16)
    pt = sb("pt", [128, 2048], BF16)
    o2s = sb("o2s", [128, 512], BF16)
    xmix = sb("xmix", [128, 4096], BF16)
    bnb = sb("bnb", [128, 28], F32)
    cbl = sb("cbl", [128, 16], F32)
    bqc = sb("bqc", [128, 8], F32)
    bkc = sb("bkc", [128, 2], F32)
    bvc = sb("bvc", [128, 2], F32)
    lng = sb("lng_s", [128, 8], F32)
    lnbb = sb("lnbb", [128, 8], BF16)
    murc = sb("murc", [128, 8], F32)
    mukc = sb("mukc", [128, 8], F32)
    murc2 = sb("murc2", [128, 8], F32)
    mukc2 = sb("mukc2", [128, 8], F32)
    tails = sb("tails", [128, 24], BF16)
    identb = sb("identb_s", [128, 128], BF16)
    trim = sb("trim_s", [128, 128], BF16)
    ones2 = sb("ones2_s", [128, 66], BF16)
    onesr = sb("onesr_s", [1, 128], BF16)
    onescol = sb("onescol_s", [128, 1], BF16)
    rzrow = sb("rzrow", [1, 512], BF16)
    rirow = sb("rirow", [1, 512], BF16)
    cwrow = sb("cwrow", [1, 1024], BF16)
    obbf = sb("obbf", [1, 1024], BF16)
    rwsc = sb("rwsc", [1, 1024], F32)
    rzbc = sb("rzbc", [128, 512], BF16)
    qstage = sb("qstage", [128, 1024], BF16)
    nexpc = sb("nexpc", [128, 1], F32)
    ps_s = es.enter_context(nc.psum_tensor("ps_s", [128, 2048], F32))
    ps_o = es.enter_context(nc.psum_tensor("ps_o", [128, 1024], F32))

    E = Emitter(nc, es)
    t, v, s, g, sp = nc.tensor, nc.vector, nc.scalar, nc.gpsimd, nc.sync
    blk = es.enter_context(nc.Block())
    scope_i = [0]

    with es:
        def gdma(dst_ap, src_ap):
            E.inc(g.dma_start(dst_ap, src_ap), "dW", 16)

        gdma(bass.AP(wq, 0, [[8192, 128], [1024, 8], [1, 1024]]),
             bass.AP(wq_d, 0, [[1024, 128], [131072, 8], [1, 1024]]))
        gdma(bass.AP(wk, 0, [[2048, 128], [256, 8], [1, 256]]),
             bass.AP(wk_d, 0, [[256, 128], [32768, 8], [1, 256]]))
        gdma(bass.AP(wv, 0, [[2048, 128], [256, 8], [1, 256]]),
             bass.AP(wv_d, 0, [[256, 128], [32768, 8], [1, 256]]))
        gdma(bass.AP(wo, 0, [[8192, 128], [1024, 8], [1, 1024]]),
             bass.AP(wo_d, 0, [[1024, 128], [131072, 8], [1, 1024]]))
        gdma(bass.AP(wr, 0, [[8192, 128], [1024, 8], [1, 1024]]),
             bass.AP(wrT_d, 0, [[1024, 128], [131072, 8], [1, 1024]]))
        gdma(bass.AP(wkf, 0, [[8192, 128], [1024, 8], [1, 1024]]),
             bass.AP(wkT_d, 0, [[1024, 128], [131072, 8], [1, 1024]]))
        gdma(bass.AP(wvf, 0, [[8192, 128], [1024, 8], [1, 1024]]),
             bass.AP(wvT_d, 0, [[1024, 128], [131072, 8], [1, 1024]]))
        gdma(lnbb[:, :], lnb_d.ap())

        def cdma(dst_ap, src_ap):
            E.inc(sp.dma_start(dst_ap, src_ap), "dC", 16)

        cdma(bqc[:, :], bq_d.ap())
        cdma(bkc[:, :], bk_d.ap())
        cdma(bvc[:, :], bv_d.ap())
        cdma(lng[:, :], lng_d.ap())
        cdma(murc[:, :], mur_d.ap())
        cdma(mukc[:, :], muk_d.ap())
        cdma(rwsc[0:1, :], bo_d.ap())
        cdma(identb[:, :], identb_d.ap())
        cdma(trim[:, :], trim_d.ap())
        cdma(ones2[:, :], ones2_d.ap())
        cdma(onesr[0:1, :], onesr_d.ap())
        cdma(onescol[:, :], onescol_d.ap())

        for eng in (t, v, s, g):
            E.wait(eng, "dW")
            E.wait(eng, "dC")

        if int(os.environ.get("KSTAGE", "99")) == -1:
            E.wait(sp, "dW")
            E.wait(sp, "dC")
            return nc

        nslot = 2 * nkvt
        vw = nkvt * VA + 32
        E.inc(g.memset(vaug[:, 0:32], 0.0), "sG")
        E.inc(g.memset(
            bass.AP(vaug, 32, [[vw, 128], [161, nslot], [1, 1]]), 1.0), "sG")
        E.inc(g.memset(
            bass.AP(vaug, 32 + 160, [[vw, 128], [161, nslot], [1, 1]]), 1.0), "sG")
        E.inc(v.memset(qstage[:, :], 0.0), "sV")
        E.inc(v.tensor_scalar(murc2[:, :], murc[:, :], -1.0, 1.0,
                              ALU.mult, ALU.add), "sV")
        E.inc(v.tensor_scalar(mukc2[:, :], mukc[:, :], -1.0, 1.0,
                              ALU.mult, ALU.add), "sV")
        gset = E.inc(g.memset(
            bass.AP(vaug, 33, [[vw, 128], [161, nslot], [1, 31]]), 0.0), "sG")
        E.inc(v.memset(tails[:, 0:8], 0.0), "sV")
        expc_v = E.inc(v.memset(nexpc[:, :], -EXPC), "sV")

        for hf in range(2):
            for cc in range(8):
                _mm = t.matmul(ps_s[0:1, hf * 512:(hf + 1) * 512],
                         lnbb[:, cc:cc + 1],
                         wo[:, cc * 1024 + hf * 512: cc * 1024 + (hf + 1) * 512],
                         start=(cc == 0), stop=(cc == 7))
        tob = E.inc(_mm, "sT")
        E.wait(v, "sT", tob)
        vv = E.inc(v.tensor_add(rwsc[0:1, :], rwsc[0:1, :], ps_s[0:1, 0:1024]), "sV")
        E.wait(v, "sV", vv)
        vv = E.inc(v.tensor_copy(obbf[0:1, :], rwsc[0:1, :]), "sV")
        E.wait(v, "sV", vv)
        for cc in range(8):
            vv = E.inc(v.tensor_scalar_mul(wo[:, cc * 1024:(cc + 1) * 1024],
                                           wo[:, cc * 1024:(cc + 1) * 1024],
                                           lng[:, cc:cc + 1]), "sV")
        vwo = vv
        E.wait(t, "sV", vwo)
        for hf in range(2):
            for cc in range(8):
                _mm = t.matmul(ps_s[0:1, 1024 + hf * 512:1024 + (hf + 1) * 512],
                         onescol[:, 0:1],
                         wo[:, cc * 1024 + hf * 512: cc * 1024 + (hf + 1) * 512],
                         start=(cc == 0), stop=(cc == 7))
        tcw = E.inc(_mm, "sT")
        E.wait(v, "sT", tcw)
        vprep = E.inc(v.tensor_copy(cwrow[0:1, :], ps_s[0:1, 1024:2048]), "sV")

        ln_state = {0: 0, 1: 0}

        def emit_ln_tiles(tiles):
            nT = len(tiles)
            stat_v = {}
            sln = {}
            out_v = []

            def phase1(i):
                p = i % 2
                xsl = tiles[i][0]
                E.wait(v, "sV", ln_state[p])
                vv = E.inc(v.bn_stats(bnb[:, p * 14:p * 14 + 6], xsl[:, 0:512]), "sV")
                E.wait(v, "sV", vv)
                vv = E.inc(v.bn_stats(bnb[:, p * 14 + 6:p * 14 + 12],
                                      xsl[:, 512:1024]), "sV")
                E.wait(v, "sV", vv)
                vv = E.inc(v.bn_aggr(bnb[:, p * 14 + 12:p * 14 + 14],
                                     bnb[:, p * 14:p * 14 + 12]), "sV")
                E.wait(v, "sV", vv)
                vv = E.inc(v.tensor_scalar_add(cbl[:, 4 * p:4 * p + 1],
                                               bnb[:, p * 14 + 13:p * 14 + 14],
                                               EPS), "sV")
                stat_v[i] = vv

            def phase2(i):
                p = i % 2
                E.wait(s, "sV", stat_v[i])
                ss_ = E.inc(s.activation(cbl[:, 4 * p + 1:4 * p + 2],
                                         cbl[:, 4 * p:4 * p + 1], ACT.Ln), "sS")
                E.wait(s, "sS", ss_)
                sln[i] = E.inc(s.activation(cbl[:, 4 * p + 2:4 * p + 3],
                                            cbl[:, 4 * p + 1:4 * p + 2],
                                            ACT.Exp, scale=-0.5), "sS")

            def phase3(i):
                p = i % 2
                xsl, outsl = tiles[i]
                E.wait(v, "sS", sln[i])
                vv = E.inc(v.scalar_tensor_tensor(cbl[:, 4 * p + 3:4 * p + 4],
                                                  bnb[:, p * 14 + 12:p * 14 + 13],
                                                  -1.0, cbl[:, 4 * p + 2:4 * p + 3],
                                                  ALU.mult, ALU.mult), "sV")
                E.wait(v, "sV", vv)
                vv = E.inc(v.tensor_scalar(outsl, xsl, cbl[:, 4 * p + 2:4 * p + 3],
                                           cbl[:, 4 * p + 3:4 * p + 4],
                                           ALU.mult, ALU.add), "sV")
                ln_state[p] = vv
                out_v.append(vv)

            phase1(0)
            phase2(0)
            for i in range(nT):
                if i + 1 < nT:
                    phase1(i + 1)
                    phase2(i + 1)
                phase3(i)
            return out_v

        def emit_ln_tile(xsl, outsl):
            return emit_ln_tiles([(xsl, outsl)])[0]

        def emit_ln_tr(tiles, dst, dst_off_fn, dst_stride, src, pre=None):
            nT = len(tiles)
            with nc.psum_tensor(f"ptr{scope_i[0]}", [128, 2048], BF16) as ptr:
                scope_i[0] += 1
                stat_v = {}
                sln = {}
                ap_v = {}
                tr_t = {}
                ev_v = {}

                def phase1(i):
                    p = i % 2
                    xsl = tiles[i][0]
                    if pre is not None:
                        pre(i)
                    E.wait(v, "sV", ln_state[p])
                    vv = E.inc(v.bn_stats(bnb[:, p * 14:p * 14 + 6],
                                          xsl[:, 0:512]), "sV")
                    E.wait(v, "sV", vv)
                    vv = E.inc(v.bn_stats(bnb[:, p * 14 + 6:p * 14 + 12],
                                          xsl[:, 512:1024]), "sV")
                    E.wait(v, "sV", vv)
                    vv = E.inc(v.bn_aggr(bnb[:, p * 14 + 12:p * 14 + 14],
                                         bnb[:, p * 14:p * 14 + 12]), "sV")
                    E.wait(v, "sV", vv)
                    vv = E.inc(v.tensor_scalar_add(cbl[:, 4 * p:4 * p + 1],
                                                   bnb[:, p * 14 + 13:p * 14 + 14],
                                                   EPS), "sV")
                    stat_v[i] = vv

                def phase2(i):
                    p = i % 2
                    E.wait(s, "sV", stat_v[i])
                    ss_ = E.inc(s.activation(cbl[:, 4 * p + 1:4 * p + 2],
                                             cbl[:, 4 * p:4 * p + 1], ACT.Ln), "sS")
                    E.wait(s, "sS", ss_)
                    sln[i] = E.inc(s.activation(cbl[:, 4 * p + 2:4 * p + 3],
                                                cbl[:, 4 * p + 1:4 * p + 2],
                                                ACT.Exp, scale=-0.5), "sS")

                def phase3(i):
                    p = i % 2
                    xsl, outsl = tiles[i]
                    E.wait(v, "sS", sln[i])
                    vv = E.inc(v.scalar_tensor_tensor(
                        cbl[:, 4 * p + 3:4 * p + 4],
                        bnb[:, p * 14 + 12:p * 14 + 13],
                        -1.0, cbl[:, 4 * p + 2:4 * p + 3],
                        ALU.mult, ALU.mult), "sV")
                    E.wait(v, "sV", vv)
                    vv = E.inc(v.tensor_scalar(outsl, xsl,
                                               cbl[:, 4 * p + 2:4 * p + 3],
                                               cbl[:, 4 * p + 3:4 * p + 4],
                                               ALU.mult, ALU.add), "sV")
                    ln_state[p] = vv
                    ap_v[i] = vv

                def trans(i):
                    p = i % 2
                    E.wait(t, "sV", ap_v[i])
                    E.wait(t, "sV", ev_v[i - 2] if i >= 2 else last_ptr_evac[0])
                    for j in range(8):
                        _mm2 = t.transpose(
                            ptr[:, p * 1024 + j * 128: p * 1024 + (j + 1) * 128],
                            src[:, i * 1024 + j * 128: i * 1024 + (j + 1) * 128],
                            identb[:, :])
                    tr_t[i] = E.inc(_mm2, "sT")

                def evac(i):
                    p = i % 2
                    E.wait(v, "sT", tr_t[i])
                    ev_v[i] = E.inc(v.tensor_copy(
                        bass.AP(dst, dst_off_fn(i),
                                [[dst.shape[1], 128], [dst_stride, 8], [1, 128]]),
                        ptr[:, p * 1024: p * 1024 + 1024]), "sV")

                phase1(0)
                phase2(0)
                for i in range(nT):
                    if i + 1 < nT:
                        phase1(i + 1)
                        phase2(i + 1)
                    phase3(i)
                    trans(i)
                    evac(i)
                last_ptr_evac[0] = ev_v[nT - 1]
            return ev_v[nT - 1]

        last_ptr_evac = [0]

        def emit_transposes(src, src_col, dst, dst_off, dst_stride, n=8,
                            src_v=None):
            with nc.psum_tensor(f"ptr{scope_i[0]}", [128, 1024], BF16) as ptr:
                scope_i[0] += 1
                if src_v is not None:
                    E.wait(t, "sV", src_v)
                E.wait(t, "sV", last_ptr_evac[0])
                for j in range(n):
                    _mm = t.transpose(ptr[:, j * 128:(j + 1) * 128],
                                src[:, src_col + j * 128: src_col + (j + 1) * 128],
                                identb[:, :])
                tt = E.inc(_mm, "sT")
                E.wait(v, "sT", tt)
                vv = E.inc(v.tensor_copy(
                    bass.AP(dst, dst_off,
                            [[dst.shape[1], 128], [dst_stride, n], [1, 128]]),
                    ptr[:, 0:n * 128]), "sV")
                last_ptr_evac[0] = vv
            return vv

        _kstage = int(os.environ.get("KSTAGE", "99"))
        if _kstage == 0:
            E.wait(sp, "dW")
            E.wait(sp, "dC")
            return nc

        n_groups = kv_len // 512
        grp_xfree = [0] * n_groups
        prev_tr_T = 0
        prev_kevac = 0
        prev_vevac = 0
        for gi in range(n_groups):
            par = gi % 2
            dxn = "dX0" if par == 0 else "dX1"
            if gi >= 2:
                E.wait(sp, "sV", grp_xfree[gi - 2])
            ld_t = []
            for ti in range(4):
                sem = f"dP{par}T{ti}"
                ld_t.append((sem, E.inc(sp.dma_start(
                    bass.AP(xb[par], ti * 1024, [[4096, 128], [1, 1024]]),
                    bass.AP(xkv_d, (gi * 512 + ti * 128) * D,
                            [[1024, 128], [1, 1024]])),
                    sem, 16)))
            E.wait(v, "sT", prev_tr_T)
            tr_done = emit_ln_tr(
                [(xb[par][:, ti * 1024:(ti + 1) * 1024],
                  sc8a[:, ti * 1024:(ti + 1) * 1024]) for ti in range(4)],
                fmA, lambda i: i * 128, 512, sc8a,
                pre=lambda i, ld_t=ld_t: E.wait(v, ld_t[i][0], ld_t[i][1]))
            prev_tr_T = E.c["sT"]
            grp_xfree[gi] = E.c["sV"]
            E.wait(t, "sV", tr_done)
            E.wait(t, "sS", prev_kevac)
            for dk2 in range(2):
                for cc in range(8):
                    _mm = t.matmul(ps_s[:, dk2 * 512:(dk2 + 1) * 512],
                             wk[:, cc * 256 + dk2 * 128: cc * 256 + (dk2 + 1) * 128],
                             fmA[:, cc * 512:(cc + 1) * 512],
                             start=(cc == 0), stop=(cc == 7))
            tk = E.inc(_mm, "sT")
            E.wait(s, "sT", tk)
            for dk2 in range(2):
                prev_kevac = E.inc(s.activation(
                    kfm[:, dk2 * kv_len + gi * 512: dk2 * kv_len + (gi + 1) * 512],
                    ps_s[:, dk2 * 512:(dk2 + 1) * 512], ACT.Identity,
                    bias=bkc[:, dk2:dk2 + 1], scale=1.0), "sS")
            E.wait(t, "sS", prev_vevac)
            for ti in range(4):
                for cc in range(8):
                    _mm = t.matmul(ps_o[:, ti * 256:(ti + 1) * 256],
                             fmA[:, cc * 512 + ti * 128: cc * 512 + (ti + 1) * 128],
                             wv[:, cc * 256:(cc + 1) * 256],
                             start=(cc == 0), stop=(cc == 7))
            tv_ = E.inc(_mm, "sT")
            E.wait(s, "sT", tv_)
            if gi == 0:
                E.wait(s, "sG", gset)
            for ti in range(4):
                tile = gi * 4 + ti
                E.inc(s.activation(
                    bass.AP(vaug, 32 + tile * VA + 32,
                            [[vw, 128], [161, 2], [1, 64]]),
                    bass.AP(ps_o, ti * 256 + 64,
                            [[1024, 128], [128, 2], [1, 64]]),
                    ACT.Identity, scale=1.0), "sS")
                prev_vevac = E.inc(s.activation(
                    bass.AP(vaug, 32 + tile * VA + 96,
                            [[vw, 128], [161, 2], [1, 64]]),
                    bass.AP(ps_o, ti * 256,
                            [[1024, 128], [128, 2], [1, 64]]),
                    ACT.Identity, scale=1.0), "sS")

        def emit_pass2(cfgs):
            if _kstage == 1:
                E.wait(sp, "sV")
                E.wait(sp, "sS")
                return nc

            for eng in (t, v, s):
                E.wait(eng, "sT")
                E.wait(eng, "sV")
                E.wait(eng, "sS")

            E.wait(sp, "sV")
            exp_done = {}
            pv_done = {}
            n_exp = [0]
            oslot_free = {0: 0, 1: 0}
            stg_guard = {0: 0, 1: 0}
            zrow_guard = {0: 0, 1: 0}
            last_s3 = [0]
            xn3f_guard = [0]
            chunk_out_dma = {0: 0, 1: 0}
            chunk_xfree = {}
            prev_wo_T = 0
            prev_sc8a_T = 0
            prev_sums_T = 0
            prev_gate_V = 0
            prev_xmix_T = {"r": 0, "k": 0}
            prev_fmB_T = 0
            prev_x1_V = 0

            if _kstage < 90:
                cfgs = cfgs[:max(0, _kstage - 1)]
            qst_guard_t = [0]
            qst_guard_v = [0]
            chunk_ld = {}

            def emit_load(ci):
                ch_ = cfgs[ci]
                nt_ = ch_["W"] // 128
                par_ = ci % 2
                dxn_ = "dX0" if par_ == 0 else "dX1"
                don_ = "dO0" if par_ == 0 else "dO1"
                E.wait(sp, don_, chunk_out_dma[par_])
                if ci >= 2:
                    E.wait(sp, "sV", chunk_xfree[ci - 2])
                chunk_ld[ci] = E.inc(sp.dma_start(
                    bass.AP(xb[par_], 0, [[4096, 128], [1024, nt_], [1, 1024]]),
                    bass.AP(xkv_d, ch_["qs"] * D,
                            [[1024, 128], [131072, nt_], [1, 1024]])),
                    dxn_, 16)

            emit_load(0)
            for ci, ch in enumerate(cfgs):
                W = ch["W"]
                nt = W // 128
                par = ci % 2
                dxn = "dX0" if par == 0 else "dX1"
                don = "dO0" if par == 0 else "dO1"
                kbmax = (ch["qs"] + W) // 128
                dq0 = ch["qs"] // 128
                ld = chunk_ld[ci]

                E.wait(v, dxn, ld)
                E.wait(v, "sT", prev_sc8a_T)
                E.wait(v, "sT", prev_xmix_T["k"])
                tr_done = emit_ln_tr(
                    [(xb[par][:, ti * 1024:(ti + 1) * 1024],
                      sc8a[:, ti * 1024:(ti + 1) * 1024]) for ti in range(nt)],
                    fmA, lambda i: i * 128, 512, sc8a)

                E.wait(t, "sV", tr_done)
                E.wait(t, "sS")
                E.wait(t, "sV")
                E.wait(t, "sT", prev_fmB_T)
                slot_evac = {}
                for qcc in range(8):
                    slot = qcc % 4
                    if qcc >= 4:
                        E.wait(t, "sS", slot_evac[slot])
                    for cc in range(8):
                        _mm = t.matmul(ps_s[:, slot * 512: slot * 512 + W],
                                 wq[:, cc * 1024 + qcc * 128: cc * 1024 + (qcc + 1) * 128],
                                 fmA[:, cc * 512: cc * 512 + W],
                                 start=(cc == 0), stop=(cc == 7))
                    tq = E.inc(_mm, "sT")
                    E.wait(s, "sT", tq)
                    slot_evac[slot] = E.inc(s.activation(
                        fmB[:, qcc * 512: qcc * 512 + W],
                        ps_s[:, slot * 512: slot * 512 + W], ACT.Identity,
                        bias=bqc[:, qcc:qcc + 1], scale=1.0), "sS")
                qevac_all = E.c["sS"]
                if ci + 1 < len(cfgs):
                    emit_load(ci + 1)

                with nc.psum_tensor(f"pso2_{scope_i[0]}", [128, 1024],
                                    F32) as ps_o2:
                    scope_i[0] += 1
                    evac_q = []

                    def obank(it_):
                        return ps_o if it_ % 2 == 0 else ps_o2

                    def evac_phase_a(p):
                        gi_ = p["gi"]
                        zrow = rzrow if gi_ == 0 else rirow
                        ob_ = p["ob"]
                        E.wait(s, "sT", p["lastpv"])
                        E.wait(s, "sS", last_s3[0])
                        E.wait(s, "sV", prev_x1_V)
                        E.wait(s, "sT", prev_wo_T)
                        E.wait(s, "sT", zrow_guard[gi_])
                        s1 = E.inc(s.activation(
                            rwsc[0:1, 0:W],
                            ob_[p["zp"]:p["zp"] + 1, p["ocol"]: p["ocol"] + W],
                            ACT.Identity, scale=1.0), "sS")
                        E.wait(s, "sS", s1)
                        s2 = E.inc(s.activation(
                            rwsc[0:1, 512:512 + W], rwsc[0:1, 0:W],
                            ACT.Ln), "sS")
                        E.wait(s, "sS", s2)
                        with nc.allow_low_precision(
                                reason="1/Z row in bf16: P is bf16 anyway"):
                            s3 = E.inc(s.activation(
                                zrow[0:1, 0:W], rwsc[0:1, 512:512 + W],
                                ACT.Exp, scale=-1.0), "sS")
                        p["s3"] = s3
                        last_s3[0] = s3

                    def evac_phase_b(p):
                        gi_ = p["gi"]
                        zrow = rzrow if gi_ == 0 else rirow
                        stg = rzbc if gi_ == 0 else o2s
                        ob_ = p["ob"]
                        rb = p["rowb"]
                        oc = p["ocol"]
                        cc2_ = p["cc2"]
                        E.wait(v, "sT", p["lastpv"])
                        E.wait(v, "sV", stg_guard[gi_])
                        if gi_ == 1:
                            E.wait(v, "sT", prev_sums_T)
                        with nc.allow_low_precision(
                                reason="O_raw staged bf16; O is bf16 anyway"):
                            c1 = E.inc(v.tensor_copy(
                                stg[rb:rb + 64, 0:W],
                                ob_[rb:rb + 64, oc: oc + W]), "sV")
                        E.wait(t, "sS", p["s3"])
                        E.wait(t, "sV", c1)
                        r1 = E.inc(t.matmul(
                            ob_[0:128, oc: oc + W],
                            onesr[0:1, :],
                            zrow[0:1, 0:W],
                            start=True, stop=True), "sT")
                        zrow_guard[gi_] = r1
                        E.wait(v, "sT", r1)
                        vv = E.inc(v.tensor_mul(
                            fmA[rb:rb + 64, cc2_ * 512: cc2_ * 512 + W],
                            stg[rb:rb + 64, 0:W],
                            ob_[rb:rb + 64, oc: oc + W]), "sV")
                        stg_guard[gi_] = vv
                        oslot_free[p["it"] % 2] = vv
                        E.wait(s, "sV", vv)
                        E.inc(s.activation(
                            fmA[rb:rb + 64, cc2_ * 512: cc2_ * 512 + W],
                            fmA[rb:rb + 64, cc2_ * 512: cc2_ * 512 + W],
                            ACT.Identity,
                            bias=bvc[rb:rb + 64, p["h"] // 2: p["h"] // 2 + 1],
                            scale=1.0), "sS")

                    E.wait(t, "sS", qevac_all)
                    jobs = []
                    itmeta = {}
                    for it2 in range(8):
                        h = it2 // 2
                        gp = it2 % 2
                        qh0 = 8 * gp + h
                        qh1 = 8 * gp + 4 + h
                        gsz = max(1, 1024 // (2 * W))
                        groups = []
                        kb = 0
                        while kb < kbmax:
                            kbs = list(range(kb, min(kb + gsz, kbmax)))
                            groups.append(kbs)
                            kb += len(kbs)
                        itmeta[it2] = dict(h=h, gq=(2 * gp, 2 * gp + 1),
                                           qcc=(qh0 // 2, qh1 // 2),
                                           rowb=64 * (h % 2),
                                           zp=64 if h % 2 == 0 else 32,
                                           ng=len(groups))
                        for ln_, kbs in enumerate(groups):
                            jobs.append((it2, ln_, kbs))
                    base0 = n_exp[0]
                    qkt_t = {}

                    it_lastq = {}
                    qcp_v = {}

                    def qslot(it_, gi):
                        if it_ % 2 == 0:
                            return qstage, gi * 512
                        return sc8a, 3072 + gi * 512

                    def emit_qcopy(it_):
                        m = itmeta[it_]
                        orow = 64 - m["rowb"]
                        E.wait(v, "sS", qevac_all)
                        E.wait(v, "sT", qst_guard_t[0])
                        E.wait(v, "sV", qst_guard_v[0])
                        if it_ >= 2:
                            E.wait(v, "sT", it_lastq.get(it_ - 2, 0))
                        for gi in (0, 1):
                            qb, c0 = qslot(it_, gi)
                            E.inc(v.memset(qb[orow:orow + 64, c0:c0 + W],
                                           0.0), "sV")
                            qcp_v[it_] = E.inc(v.tensor_copy(
                                qb[m["rowb"]:m["rowb"] + 64, c0:c0 + W],
                                fmB[m["rowb"]:m["rowb"] + 64,
                                    m["qcc"][gi] * 512: m["qcc"][gi] * 512 + W]),
                                "sV")

                    def emit_qkt_job(k):
                        it_, ln2, kbs_ = jobs[k]
                        m = itmeta[it_]
                        n = base0 + k
                        sp2 = n % 2
                        if ln2 == 0:
                            emit_qcopy(it_)
                            E.wait(t, "sV", qcp_v[it_])
                        if n >= 2:
                            E.wait(t, "sS", exp_done[n - 2])
                        for j, kbj in enumerate(kbs_):
                            stat = kfm[:, (m["h"] // 2) * kv_len + kbj * 128:
                                       (m["h"] // 2) * kv_len + (kbj + 1) * 128]
                            for gi in (0, 1):
                                qb, c0 = qslot(it_, gi)
                                _mm = t.matmul(
                                    ps_s[:, sp2 * 1024 + (2 * j + gi) * W:
                                         sp2 * 1024 + (2 * j + gi + 1) * W],
                                    stat,
                                    qb[:, c0: c0 + W],
                                    start=True, stop=True)
                        qkt_t[n] = E.inc(_mm, "sT")
                        it_lastq[it_] = qkt_t[n]

                    def ptsl(n2):
                        sl = n2 % 3
                        if sl == 0:
                            return pt, 0, 2048
                        if sl == 1:
                            return pt, 1024, 2048
                        return xn3f, 0, 4104

                    needs = {}

                    def emit_pv(k2, bk):
                        it_, ln2, kbs_ = jobs[k2]
                        m2 = itmeta[it_]
                        n2 = base0 + k2
                        pb, pbase, _pp = ptsl(n2)
                        ob_ = obank(it_)
                        if ln2 == 0 and it_ >= 2:
                            while evac_q:
                                _sl, fn_, rec_ = evac_q.pop(0)
                                fn_(rec_)
                            E.wait(t, "sV", oslot_free[it_ % 2])
                        E.wait(t, "sS", exp_done[n2])
                        nv, ng2 = needs[n2]
                        if nv:
                            E.wait(t, "sV", nv)
                        if ng2:
                            E.wait(t, "sG", ng2)
                        for j, kbj in enumerate(kbs_):
                            sbase = 32 + kbj * VA + (m2["h"] // 2) * 161
                            if m2["h"] % 2 == 0:
                                stat = vaug[:, sbase + 96: sbase + 161]
                                np_ = 65
                            else:
                                stat = vaug[:, sbase - 32: sbase + 96]
                                np_ = 128
                            for gi in (0, 1):
                                _mm = t.matmul(
                                    ob_[0:np_, gi * 512: gi * 512 + W],
                                    stat,
                                    pb[:, pbase + (2 * j + gi) * W:
                                       pbase + (2 * j + gi + 1) * W],
                                    start=(ln2 == 0 and j == 0),
                                    stop=(kbj == kbmax - 1))
                        pv_done[n2] = E.inc(_mm, "sT")
                        if ln2 == m2["ng"] - 1:
                            recs = []
                            for gi in (0, 1):
                                recs.append(dict(
                                    h=m2["h"], gi=gi, it=it_, ob=ob_,
                                    ocol=gi * 512,
                                    zp=m2["zp"], rowb=m2["rowb"],
                                    cc2=2 * m2["gq"][gi] + m2["h"] // 2,
                                    lastpv=pv_done[n2]))
                            evac_q.extend([
                                (bk + 1, evac_phase_a, recs[0]),
                                (bk + 1, evac_phase_a, recs[1]),
                                (bk + 2, evac_phase_b, recs[0]),
                                (bk + 2, evac_phase_b, recs[1])])

                    emit_qkt_job(0)
                    for k, (it2, ln_, kbs) in enumerate(jobs):
                        m = itmeta[it2]
                        n = base0 + k
                        sp2 = n % 2
                        pb, pbase, ppitch = ptsl(n)
                        if k + 1 < len(jobs):
                            emit_qkt_job(k + 1)
                        E.wait(s, "sT", qkt_t[n])
                        if n >= 3:
                            E.wait(s, "sT", pv_done[n - 3])
                        if n % 3 == 2:
                            E.wait(s, "sV", xn3f_guard[0])
                        exp_done[n] = E.inc(s.activation(
                            pb[:, pbase: pbase + 2 * len(kbs) * W],
                            ps_s[:, sp2 * 1024: sp2 * 1024 + 2 * len(kbs) * W],
                            ACT.Exp, bias=nexpc[:, 0:1], scale=0.125), "sS")
                        need_v = 0
                        need_g = 0
                        for j, kbj in enumerate(kbs):
                            if kbj >= dq0:
                                d = kbj * 128 - ch["qs"]
                                base = pbase + 2 * j * W
                                E.wait(v, "sS", exp_done[n])
                                for gi in (0, 1):
                                    need_v = E.inc(v.tensor_mul(
                                        pb[:, base + gi * W + d:
                                           base + gi * W + d + 128],
                                        pb[:, base + gi * W + d:
                                           base + gi * W + d + 128],
                                        trim[:, :]), "sV")
                                if d > 0:
                                    E.wait(g, "sS", exp_done[n])
                                    need_g = E.inc(g.memset(
                                        bass.AP(pb, base,
                                                [[ppitch, 128], [W, 2], [1, d]]),
                                        0.0), "sG")
                        needs[n] = (need_v, need_g)
                        if k >= 2:
                            emit_pv(k - 2, k)
                        while evac_q and evac_q[0][0] <= k:
                            _sl, fn_, rec_ = evac_q.pop(0)
                            fn_(rec_)
                    for k2_ in range(max(0, len(jobs) - 2), len(jobs)):
                        emit_pv(k2_, len(jobs))
                    n_exp[0] = base0 + len(jobs)
                    while evac_q:
                        _sl, fn_, rec_ = evac_q.pop(0)
                        fn_(rec_)
                attn_S = E.c["sS"]
                prev_fmB_T = E.c["sT"]

                with nc.psum_tensor(f"psst{scope_i[0]}", [33, 512], F32) as psst, \
                     nc.psum_tensor(f"pstc{scope_i[0]}", [128, 16], BF16) as pstc:
                    scope_i[0] += 1
                    sq_s = {}
                    mm_t = {}
                    E.wait(s, "sS", attn_S)
                    E.wait(s, "sT", prev_sums_T)
                    E.wait(s, "sV", prev_gate_V)
                    E.wait(t, "sS", attn_S)
                    for cc2 in range(8):
                        if cc2 >= 1:
                            E.wait(s, "sT", mm_t[cc2 - 1])
                        sq_s[cc2] = E.inc(s.activation(
                            o2s[:, 0:W],
                            fmA[:, cc2 * 512: cc2 * 512 + W], ACT.Square), "sS")
                        _mm = t.matmul(psst[0:33, 0:W], ones2[:, 0:33],
                                 fmA[:, cc2 * 512: cc2 * 512 + W],
                                 start=(cc2 == 0), stop=False)
                        E.wait(t, "sS", sq_s[cc2])
                        _mm = t.matmul(psst[0:33, 0:W], ones2[:, 33:66],
                                 o2s[:, 0:W],
                                 start=False, stop=(cc2 == 7))
                        mm_t[cc2] = E.inc(_mm, "sT")
                    tst = mm_t[7]
                    prev_sums_T = tst
                    E.wait(v, "sT", tst)
                    vv = E.inc(v.tensor_scalar_mul(rzrow[0:1, 0:W], psst[0:1, 0:W],
                                                   -1.0 / D), "sV")
                    E.wait(v, "sV", vv)
                    vv = E.inc(v.tensor_scalar_mul(rwsc[0:1, 0:W], psst[0:1, 0:W],
                                                   1.0 / D), "sV")
                    E.wait(v, "sV", vv)
                    vv = E.inc(v.tensor_mul(rwsc[0:1, 512:512 + W],
                                            rwsc[0:1, 0:W], rwsc[0:1, 0:W]), "sV")
                    E.wait(v, "sV", vv)
                    vv = E.inc(v.scalar_tensor_tensor(
                        rwsc[0:1, 0:W], psst[32:33, 0:W], 1.0 / D,
                        rwsc[0:1, 512:512 + W], ALU.mult, ALU.subtract), "sV")
                    E.wait(v, "sV", vv)
                    vv = E.inc(v.tensor_scalar_add(rwsc[0:1, 0:W],
                                                   rwsc[0:1, 0:W], EPS), "sV")
                    E.wait(s, "sV", vv)
                    ssl = E.inc(s.activation(rwsc[0:1, 512:512 + W],
                                             rwsc[0:1, 0:W], ACT.Ln), "sS")
                    E.wait(s, "sS", ssl)
                    ri_s = E.inc(s.activation(rirow[0:1, 0:W], rwsc[0:1, 512:512 + W],
                                              ACT.Exp, scale=0.5), "sS")
                    E.wait(v, "sT", tst)
                    vv0 = E.inc(v.tensor_copy(qstage[0:1, 512:512 + W], psst[0:1, 0:W]), "sV")
                    E.wait(v, "sV", vv0)
                    vsf = E.inc(v.tensor_copy(qstage[32:33, 512:512 + W], psst[32:33, 0:W]), "sV")
                    E.wait(t, "sV", vsf)
                    for ti in range(nt):
                        _mm = t.transpose(pstc[:, ti * 4:ti * 4 + 1],
                                    qstage[0:1, 512 + ti * 128:512 + (ti + 1) * 128], identb[0:1, 0:1])
                        _mm = t.transpose(pstc[:, ti * 4 + 2:ti * 4 + 3],
                                    qstage[32:33, 512 + ti * 128:512 + (ti + 1) * 128],
                                    identb[32:33, 32:33])
                    ttc = E.inc(_mm, "sT")
                    qst_guard_t[0] = ttc
                    rr_s = {}
                    ln_s = {}
                    for ti in range(nt):
                        E.wait(v, "sT", ttc)
                        if ti >= 1:
                            E.wait(v, "sS", ln_s[ti - 1])
                        vv = E.inc(v.tensor_scalar_mul(cbl[:, 8:9],
                                                       pstc[:, ti * 4:ti * 4 + 1],
                                                       1.0 / D), "sV")
                        E.wait(v, "sV", vv)
                        vv = E.inc(v.tensor_mul(cbl[:, 9:10],
                                                cbl[:, 8:9], cbl[:, 8:9]), "sV")
                        E.wait(v, "sV", vv)
                        vv = E.inc(v.scalar_tensor_tensor(
                            cbl[:, 10:11], pstc[:, ti * 4 + 2:ti * 4 + 3], 1.0 / D,
                            cbl[:, 9:10], ALU.mult, ALU.subtract), "sV")
                        E.wait(v, "sV", vv)
                        vv = E.inc(v.tensor_scalar_add(cbl[:, 10:11],
                                                       cbl[:, 10:11], EPS), "sV")
                        E.wait(s, "sV", vv)
                        if ti >= 1:
                            E.wait(s, "sS", rr_s[ti - 1])
                        ln_s[ti] = E.inc(s.activation(cbl[:, 11:12], cbl[:, 10:11],
                                                      ACT.Ln), "sS")
                        E.wait(s, "sS", ln_s[ti])
                        rr_s[ti] = E.inc(s.activation(cbl[:, 12 + ti:13 + ti],
                                                      cbl[:, 11:12],
                                                      ACT.Exp, scale=-0.5), "sS")
                    vmu_done = E.c["sV"]
                    x1_done = {}
                    for ti in range(nt):
                        wslot = ti % 2
                        if ti >= 2:
                            E.wait(t, "sV", x1_done[ti - 2])
                        if ti == 0:
                            E.wait(t, "sS", attn_S)
                        for hf in range(2):
                            pss = ps_s[:, wslot * 1024 + hf * 512: wslot * 1024 + (hf + 1) * 512]
                            for cc2 in range(8):
                                _mm = t.matmul(pss,
                                         fmA[:, cc2 * 512 + ti * 128: cc2 * 512 + (ti + 1) * 128],
                                         wo[:, cc2 * 1024 + hf * 512: cc2 * 1024 + (hf + 1) * 512],
                                         start=(cc2 == 0), stop=False)
                            E.wait(t, "sV", vmu_done)
                            E.wait(t, "sS", ri_s)
                            _mm = t.matmul(pss, rzrow[0:1, ti * 128:(ti + 1) * 128],
                                     cwrow[0:1, hf * 512:(hf + 1) * 512],
                                     start=False, stop=False)
                            _mm = t.matmul(pss, rirow[0:1, ti * 128:(ti + 1) * 128],
                                     obbf[0:1, hf * 512:(hf + 1) * 512],
                                     start=False, stop=True)
                        two = E.inc(_mm, "sT")
                        E.wait(v, "sT", two)
                        E.wait(v, "sS", rr_s[ti])
                        x1_done[ti] = E.inc(v.scalar_tensor_tensor(
                            xb[par][:, ti * 1024:(ti + 1) * 1024],
                            ps_s[:, wslot * 1024:(wslot + 1) * 1024],
                            cbl[:, 12 + ti:13 + ti],
                            xb[par][:, ti * 1024:(ti + 1) * 1024],
                            ALU.mult, ALU.add), "sV")
                    prev_wo_T = E.c["sT"]
                    prev_x1_V = E.c["sV"]

                E.wait(v, "sV", prev_x1_V)
                E.wait(v, "sT", prev_fmB_T)
                E.wait(v, "sS", attn_S)
                if not ch["ghost"]:
                    ssl_ = TAIL_SLOT[ch["tail_src"]]
                    vv = E.inc(v.tensor_copy(
                        bass.AP(xn3f, 0, [[4104, 128], [513, 8], [1, 1]]),
                        bass.AP(tails, ssl_ * 8, [[24, 128], [1, 8]])), "sV")
                    tails_in_v = vv
                tr3 = emit_ln_tr(
                    [(xb[par][:, ti * 1024:(ti + 1) * 1024],
                      sc8a[:, ti * 1024:(ti + 1) * 1024]) for ti in range(nt)],
                    xn3f, lambda i: 1 + i * 128, 513, sc8a)
                prev_sc8a_T = E.c["sT"]
                chunk_xfree[ci] = E.c["sV"]
                if ch["tail_dst"] is not None:
                    dsl_ = TAIL_SLOT[ch["tail_dst"]]
                    E.wait(v, "sV", tr3)
                    E.inc(v.tensor_copy(
                        bass.AP(tails, dsl_ * 8, [[24, 128], [1, 8]]),
                        bass.AP(xn3f, W, [[4104, 128], [513, 8], [1, 1]])), "sV")
                xn3f_guard[0] = E.c["sV"]
                if ch["ghost"]:
                    continue

                def emit_mix(mu, mu2, dstbuf=None):
                    dstbuf = xmix if dstbuf is None else dstbuf
                    last = 0
                    for cc in range(8):
                        vv = E.inc(v.tensor_scalar_mul(
                            dstbuf[:, cc * 512: cc * 512 + W],
                            xn3f[:, cc * 513: cc * 513 + W],
                            mu2[:, cc:cc + 1]), "sV")
                        E.wait(v, "sV", vv)
                        last = E.inc(v.scalar_tensor_tensor(
                            dstbuf[:, cc * 512: cc * 512 + W],
                            xn3f[:, cc * 513 + 1: cc * 513 + 1 + W],
                            mu[:, cc:cc + 1],
                            dstbuf[:, cc * 512: cc * 512 + W],
                            ALU.mult, ALU.add), "sV")
                    return last

                E.wait(v, "sV", tr3)
                E.wait(v, "sV", tails_in_v)
                E.wait(v, "sT", prev_xmix_T["k"])
                mix_r = emit_mix(murc, murc2)
                E.wait(t, "sV", mix_r)
                E.wait(t, "sV", prev_x1_V)
                sig_s = {}
                for ti in range(nt):
                    wslot = ti % 2
                    if ti >= 2:
                        E.wait(t, "sS", sig_s[ti - 2])
                    for hf in range(2):
                        pss = ps_s[:, wslot * 1024 + hf * 512: wslot * 1024 + (hf + 1) * 512]
                        for cc in range(8):
                            _mm = t.matmul(pss,
                                     xmix[:, cc * 512 + ti * 128: cc * 512 + (ti + 1) * 128],
                                     wr[:, cc * 1024 + hf * 512: cc * 1024 + (hf + 1) * 512],
                                     start=(cc == 0), stop=(cc == 7))
                    tr_ = E.inc(_mm, "sT")
                    E.wait(s, "sT", tr_)
                    E.wait(s, "sT", prev_sc8a_T)
                    sig_s[ti] = E.inc(s.activation(
                        sc8a[:, ti * 1024:(ti + 1) * 1024],
                        ps_s[:, wslot * 1024:(wslot + 1) * 1024], ACT.Sigmoid), "sS")
                prev_xmix_T["r"] = E.c["sT"]
                E.wait(v, "sT", prev_wo_T)
                mix_k = emit_mix(mukc, mukc2, dstbuf=fmA)
                E.wait(t, "sV", mix_k)
                E.wait(t, "sS", sig_s[nt - 1])
                kact_v = {}
                for kcc in range(8):
                    slot = kcc % 4
                    if kcc >= 4:
                        E.wait(t, "sV", kact_v[kcc - 4])
                    for cc in range(8):
                        _mm = t.matmul(ps_s[:, slot * 512: slot * 512 + W],
                                 wkf[:, cc * 1024 + kcc * 128: cc * 1024 + (kcc + 1) * 128],
                                 fmA[:, cc * 512: cc * 512 + W],
                                 start=(cc == 0), stop=(cc == 7))
                    tk_ = E.inc(_mm, "sT")
                    E.wait(v, "sT", tk_)
                    if kcc >= 1:
                        E.wait(v, "sV", kact_v[kcc - 1])
                    vv = E.inc(v.tensor_scalar_max(
                        qstage[:, 0:W], ps_s[:, slot * 512: slot * 512 + W], 0.0), "sV")
                    E.wait(v, "sV", vv)
                    kact_v[kcc] = E.inc(v.tensor_mul(
                        fmB[:, kcc * 512: kcc * 512 + W],
                        qstage[:, 0:W], qstage[:, 0:W]), "sV")
                prev_xmix_T["k"] = E.c["sT"]
                qst_guard_v[0] = E.c["sV"]
                gate_v = {}
                for ti in range(nt):
                    vslot = ti % 2
                    E.wait(t, "sV", kact_v[7])
                    if ti >= 2:
                        E.wait(t, "sV", gate_v[ti - 2])
                    for hf in range(2):
                        if vslot == 0:
                            pss = ps_o[:, hf * 512:(hf + 1) * 512]
                        else:
                            pss = ps_s[:, hf * 512:(hf + 1) * 512]
                        for cc in range(8):
                            _mm = t.matmul(pss,
                                     fmB[:, cc * 512 + ti * 128: cc * 512 + (ti + 1) * 128],
                                     wvf[:, cc * 1024 + hf * 512: cc * 1024 + (hf + 1) * 512],
                                     start=(cc == 0), stop=(cc == 7))
                    tvv = E.inc(_mm, "sT")
                    E.wait(v, "sT", tvv)
                    E.wait(v, "sS", sig_s[ti])
                    E.wait(v, "sT", prev_sums_T)
                    if ti >= 1:
                        E.wait(v, "sV", gate_v[ti - 1])
                    psrc = ps_o if vslot == 0 else ps_s
                    for hf in range(2):
                        vv = E.inc(v.tensor_mul(
                            o2s[:, 0:512],
                            sc8a[:, ti * 1024 + hf * 512: ti * 1024 + (hf + 1) * 512],
                            psrc[:, hf * 512:(hf + 1) * 512]), "sV")
                        E.wait(v, "sV", vv)
                        gate_v[ti] = E.inc(v.tensor_add(
                            xb[par][:, ti * 1024 + hf * 512: ti * 1024 + (hf + 1) * 512],
                            xb[par][:, ti * 1024 + hf * 512: ti * 1024 + (hf + 1) * 512],
                            o2s[:, 0:512]), "sV")
                        E.wait(v, "sV", gate_v[ti])
                prev_gate_V = E.c["sV"]
                chunk_xfree[ci] = E.c["sV"]
                qst_guard_v[0] = E.c["sV"]
                xn3f_guard[0] = E.c["sV"]

                E.wait(sp, "sV", prev_gate_V)
                chunk_out_dma[par] = E.inc(sp.dma_start(
                    bass.AP(out_d, ch["out_row"] * D,
                            [[1024, 128], [131072, nt], [1, 1024]]),
                    bass.AP(xb[par], 0, [[4096, 128], [1024, nt], [1, 1024]])),
                    don, 16)

            E.wait(sp, "dO0")
            E.wait(sp, "dO1")

        if role == "switch":
            engs = [t, v, s, g, sp]
            idx = [e.partition_id() % 2 for e in engs]
            snap = dict(E.c)
            snap_ptr = last_ptr_evac[0]
            snap_ln = dict(ln_state)
            for _ri in nc.Switch(engines=engs, index=idx, n=2):
                E.c.clear()
                E.c.update(snap)
                last_ptr_evac[0] = snap_ptr
                ln_state.clear()
                ln_state.update(snap_ln)
                emit_pass2(chunk_cfgs(_ri, seq_len)[0])
        else:
            emit_pass2(cfgs)

    return nc



def _pcol(vec):
    vec = np.asarray(vec, np.float32)
    return np.ascontiguousarray(vec.reshape(-1, 128).T)


def _consts():
    identb = np.eye(128, dtype=ml_dtypes.bfloat16)
    trim = (np.tril(np.ones((128, 128), np.float32)).T).astype(ml_dtypes.bfloat16)
    ones2 = np.zeros((128, 66), np.float32)
    ones2[:, 0] = 1.0
    ones2[:, 33 + 32] = 1.0
    ones2 = ones2.astype(ml_dtypes.bfloat16)
    onesr = np.ones((1, 128), ml_dtypes.bfloat16)
    onescol = np.ones((128, 1), ml_dtypes.bfloat16)
    return dict(identb=identb, trim=trim, ones2=ones2,
                onesr=onesr, onescol=onescol)


def kernel(L0_attn_x, L0_attn_Wq, L0_attn_bq, L0_attn_Wk, L0_attn_bk,
           L0_attn_Wv, L0_attn_bv, L0_attn_ln_g, L0_attn_ln_b, L0_attn_Wo,
           L0_attn_bo, L0_ffn_W_r, L0_ffn_W_k, L0_ffn_W_v, L0_ffn_mu_r,
           L0_ffn_mu_k):
    x = np.ascontiguousarray(np.asarray(L0_attn_x, np.float32))
    B, N, _ = x.shape
    shared = dict(
        wq=np.ascontiguousarray(L0_attn_Wq, np.float32),
        wk=np.ascontiguousarray(L0_attn_Wk, np.float32),
        wv=np.ascontiguousarray(L0_attn_Wv, np.float32),
        wo=np.ascontiguousarray(L0_attn_Wo, np.float32),
        wrT=np.ascontiguousarray(np.asarray(L0_ffn_W_r, np.float32).T),
        wkT=np.ascontiguousarray(np.asarray(L0_ffn_W_k, np.float32).T),
        wvT=np.ascontiguousarray(np.asarray(L0_ffn_W_v, np.float32).T),
        lng=_pcol(L0_attn_ln_g), lnb=_pcol(L0_attn_ln_b),
        bq=_pcol(L0_attn_bq), bk=_pcol(L0_attn_bk), bv=_pcol(L0_attn_bv),
        bo=np.ascontiguousarray(np.asarray(L0_attn_bo, np.float32).reshape(1, D)),
        mur=_pcol(L0_ffn_mu_r), muk=_pcol(L0_ffn_mu_k),
        **_consts(),
    )
    nc = build_nc(role="switch", seq_len=N)
    in_maps = []
    for c in range(8):
        b = c // 2
        m = dict(shared)
        m["xkv"] = x[b]
        in_maps.append(m)
    res = run_bass_kernel_spmd(nc, in_maps, core_ids=list(range(8)))
    out = np.empty((B, N, D), np.float32)
    for b in range(B):
        r0 = res.results[2 * b]["out"]
        r1 = res.results[2 * b + 1]["out"]
        out[b, 0:1024] = r0[0:1024]
        out[b, 3072:4096] = r0[1024:2048]
        out[b, 1024:3072] = r1[0:2048]
    return out
